# version 04
# baseline (speedup 1.0000x reference)
import math
import re
import numpy as np
import ml_dtypes
import concourse.bass as bass
import concourse.mybir as mybir
from concourse.bass_utils import run_bass_kernel_spmd
from contextlib import ExitStack

F32 = mybir.dt.float32
BF16 = mybir.dt.bfloat16
I32 = mybir.dt.int32
AF = mybir.ActivationFunctionType
ALU = mybir.AluOpType
AX = mybir.AxisListType
BF = ml_dtypes.bfloat16

NCORES = 8
T = 4096
D = 1024
NC = D // 128
TT = 512
NT = T // TT
QT = 512
NQ = T // QT
LN_EPS = 1e-5

ENGINES = ("sync", "scalar", "vector", "tensor", "gpsimd")
FIFO_EDGES = False


class Sem:
    __slots__ = ("h", "n", "name")

    def __init__(self, h, name):
        self.h = h
        self.n = 0
        self.name = name


class Ev:
    __slots__ = ("sem", "val", "eng", "dma")

    def __init__(self, sem, val, eng, dma):
        self.sem = sem
        self.val = val
        self.eng = eng
        self.dma = dma


class KB:

    def __init__(self, nc):
        self.nc = nc
        self.ctx = ExitStack()
        self.ops = {e: [] for e in ENGINES}
        self.qsem = {e: None for e in ENGINES}
        self.streams = {}
        self.qstreams = {e: [] for e in ENGINES}
        self.nsem = 0
        self.qlast = {}

    def sem(self, name=None):
        name = name or f"s{self.nsem}"
        self.nsem += 1
        return Sem(self.ctx.enter_context(self.nc.semaphore(name)), name)

    def _q(self, eng):
        if self.qsem[eng] is None:
            self.qsem[eng] = self.sem(f"q_{eng}")
        return self.qsem[eng]

    def sb(self, name, shape, dtype):
        return self.ctx.enter_context(self.nc.sbuf_tensor(name, shape, dtype))

    def ps(self, name, shape, dtype=F32):
        return self.ctx.enter_context(self.nc.psum_tensor(name, shape, dtype))

    def op(self, eng, fn, wait=(), inc=None, amt=1, strict=False, fence=False,
           _dma_sem=None):
        evs = []

        def _flat(w):
            if w is None:
                return
            if isinstance(w, (list, tuple)):
                for x in w:
                    _flat(x)
            elif w.dma:
                evs.append(Ev(w.sem, w.sem.n, w.eng, True))
            else:
                evs.append(w)
        _flat(list(wait) if isinstance(wait, (list, tuple)) else [wait])
        if FIFO_EDGES and _dma_sem is None and self.qlast.get(eng) is not None:
            ql = self.qlast[eng]
            evs.append(Ev(ql.sem, ql.val, "__fifo__", False))
        s = self._q(eng)
        if fence and s.n > 0:
            evs.append(Ev(s, s.n, "__fence__", True))
        if _dma_sem is not None:
            _dma_sem.n += amt
            ev = Ev(_dma_sem, _dma_sem.n, eng, True)
            self.ops[eng].append((fn, evs, (_dma_sem, amt)))
            return ev
        s.n += amt
        ev = Ev(s, s.n, eng, amt >= 16 or strict)
        self.ops[eng].append((fn, evs, (s, amt)))
        self.qlast[eng] = ev
        return ev

    def dma(self, q, out, in_, wait=(), inc=None, stream=None, **kw):
        if stream is None:
            try:
                o_sb = out.space == bass.MemorySpace.SBUF
            except Exception:
                o_sb = False
            nm = out.name if o_sb else "st_" + out.name
            stream = re.sub(r"_[0-9]+$", "", nm)
        key = (q, stream)
        if key not in self.streams:
            self.streams[key] = self.sem(f"d_{key[0]}_{key[1]}"[:30])
            self.qstreams[q].append(self.streams[key])
        return self.op(q, lambda e: e.dma_start(out=out, in_=in_, **kw),
                       wait, None, 16, _dma_sem=self.streams[key])

    def queue_done(self, eng):
        evs = [Ev(s, s.n, "__fin__", True) for s in self.qstreams[eng] if s.n]
        s = self._q(eng)
        if s.n:
            evs.append(Ev(s, s.n, "__fin__", True))
        return evs

    def barrier(self, engines):
        return [self.op(e, lambda en: en.nop()) for e in engines]

    def build(self):
        waitmax = {}

        def mk_body(eng, items):
            def body(e):
                for (fn, evs, inc) in items:
                    for ev in evs:
                        if ev.eng == eng and not ev.dma:
                            continue
                        key = (eng, ev.sem.name)
                        if waitmax.get(key, -1) >= ev.val:
                            continue
                        e.wait_ge(ev.sem.h, ev.val)
                        waitmax[key] = ev.val
                    inst = fn(e)
                    assert inst is not None, "op fn returned None (compound op?)"
                    if inc is not None:
                        inst.then_inc(inc[0].h, inc[1])
            return body

        with self.nc.Block() as block:
            for eng in ENGINES:
                if self.ops[eng]:
                    getattr(block, eng)(mk_body(eng, self.ops[eng]))
        self.ctx.close()



class PostLN:

    def __init__(self, k, name, psum):
        self.k = k
        self.ps = psum
        self.mrow = k.sb(f"{name}_m", [1, TT], F32)
        self.vrow = k.sb(f"{name}_v", [1, TT], F32)
        self.lrow = k.sb(f"{name}_l", [1, TT], F32)
        self.rrow = k.sb(f"{name}_r", [1, TT], F32)
        self.mrrow = k.sb(f"{name}_mr", [1, TT], F32)
        self.rb = k.sb(f"{name}_rb", [128, TT], F32)
        self.mrb = k.sb(f"{name}_mrb", [128, TT], F32)
        self.sqt = [k.sb(f"{name}_sq{i}", [128, TT], F32) for i in range(2)]
        self.xt = [k.sb(f"{name}_x{i}", [128, TT], F32) for i in range(2)]
        self.ones_col = k.sb(f"{name}_oc", [128, 1], F32)
        self.ones_row = k.sb(f"{name}_or", [1, 128], F32)
        self.eps_row = k.sb(f"{name}_eps", [1, 1], F32)
        self.e_oc = k.op("vector", lambda e: e.memset(self.ones_col[:], 1.0))
        self.e_or = k.op("vector", lambda e: e.memset(self.ones_row[:], 1.0))
        self.e_eps = k.op("vector", lambda e: e.memset(self.eps_row[:], LN_EPS))
        self.e_stats_read = None
        self.e_sq_mm_prev = None
        self.e_r_prev = None
        self.e_b1_prev = None
        self.e_b2_prev = None
        self.e_mr_prev = None
        self.war = {}

    def tile(self, zrows, z_evs, sink):
        k = self.k
        e_sum_mm = None
        e_sq_mm = None
        e_sqmm_by = {}
        for dout in range(NC):
            e_sum_mm = k.op("tensor", lambda e, z=zrows[dout], dout=dout: e.matmul(
                self.ps["sum"], self.ones_col[:], z,
                start=(dout == 0), stop=(dout == NC - 1)),
                wait=[z_evs[dout], self.e_oc] +
                     ([self.e_stats_read] if dout == 0 else []))
        for dout in range(NC):
            e_sq = k.op("vector", lambda e, z=zrows[dout], b=self.sqt[dout % 2]:
                        e.tensor_mul(b[:], z, z),
                        wait=[z_evs[dout], e_sqmm_by.get(dout - 2),
                              self.e_sq_mm_prev if dout < 2 else None])
            e_sq_mm = k.op("tensor", lambda e, dout=dout, b=self.sqt[dout % 2]:
                           e.matmul(self.ps["sq"], self.ones_col[:], b[:],
                                    start=(dout == 0), stop=(dout == NC - 1)),
                           wait=[e_sq] + ([self.e_stats_read] if dout == 0 else []))
            e_sqmm_by[dout] = e_sq_mm
        self.e_sq_mm_prev = e_sq_mm
        e_m = k.op("vector", lambda e: e.tensor_scalar_mul(
            self.mrow[:], self.ps["sum"], 1.0 / D),
            wait=[e_sum_mm, self.e_mr_prev])
        e_v1 = k.op("vector", lambda e: e.tensor_scalar_mul(
            self.vrow[:], self.ps["sq"], 1.0 / D),
            wait=[e_sq_mm, self.e_r_prev])
        e_v2 = k.op("vector", lambda e: e.tensor_mul(
            self.lrow[:], self.mrow[:], self.mrow[:]), wait=[self.e_r_prev])
        e_v3 = k.op("vector", lambda e: e.tensor_sub(
            self.vrow[:], self.vrow[:], self.lrow[:]))
        self.e_stats_read = e_v3
        e_l1 = k.op("scalar", lambda e: e.activation(
            self.lrow[:], self.vrow[:], AF.Ln, bias=self.eps_row[:, 0:1]),
            wait=[e_v3, self.e_eps])
        e_r = k.op("scalar", lambda e: e.activation(
            self.rrow[:], self.lrow[:], AF.Exp, scale=-0.5),
            wait=[self.e_mr_prev, self.e_b1_prev])
        self.e_r_prev = e_r
        e_mr = k.op("vector", lambda e: e.tensor_mul(
            self.mrrow[:], self.mrow[:], self.rrow[:]),
            wait=[e_r, self.e_b2_prev])
        self.e_mr_prev = e_mr
        e_b1 = k.op("tensor", lambda e: e.matmul(
            self.ps["b1"], self.ones_row[:], self.rrow[:], start=True, stop=True),
            wait=[e_r, self.e_or, self.e_b1_prev])
        e_rb = k.op("vector", lambda e: e.tensor_copy(self.rb[:], self.ps["b1"]),
                    wait=[e_b1])
        self.e_b1_prev = e_rb
        e_b2 = k.op("tensor", lambda e: e.matmul(
            self.ps["b2"], self.ones_row[:], self.mrrow[:], start=True, stop=True),
            wait=[e_mr, self.e_or, self.e_b2_prev, e_rb])
        e_mrb = k.op("vector", lambda e: e.tensor_copy(self.mrb[:], self.ps["b2"]),
                     wait=[e_b2])
        self.e_b2_prev = e_mrb
        norm_evs = []
        for dout in range(NC):
            xt = self.xt[dout % 2]
            e_n1 = k.op("vector", lambda e, z=zrows[dout], xt=xt: e.tensor_mul(
                xt[:], z, self.rb[:]),
                wait=[e_rb, self.war.get(dout % 2)])
            e_n2 = k.op("vector", lambda e, xt=xt: e.tensor_sub(
                xt[:], xt[:], self.mrb[:]), wait=[e_mrb])
            self.war[dout % 2] = sink(dout, xt, e_n2)
            norm_evs.append(e_n2)
        return e_v3, norm_evs



def emit_A1(k, io):
    nc = k.nc
    dbg = io.get("dbg")

    x_bf = [k.sb(f"xbf{c}", [128, T + 1], BF16) for c in range(NC)]
    ksb = k.sb("ksb", [128, T], F32)
    vsb = k.sb("vsb", [128, T], F32)
    srsb = k.sb("srsb", [128, T], F32)
    w_ek = k.sb("w_ek", [128, QT], F32)
    w_ekv = k.sb("w_ekv", [128, QT], F32)
    w_A = k.sb("w_A", [128, QT + 1], F32)
    w_B = k.sb("w_B", [128, QT + 1], F32)
    w_n = k.sb("w_n", [128, QT], F32)
    w_d2 = k.sb("w_d2", [128, QT], F32)
    rwkv_bf = k.sb("rwkv_bf", [128, T], BF16)
    stA = k.sb("stA", [128, 1], F32)
    stB = k.sb("stB", [128, 1], F32)
    ew_all = k.sb("ew_all", [128, NC], F32)
    u_all = k.sb("u_all", [128, NC], F32)
    tm_all = {p: k.sb(f"tm_{p}", [128, NC], F32) for p in ("k", "v", "r")}
    tm1m = {p: k.sb(f"tm1m_{p}", [128, NC], F32) for p in ("k", "v", "r")}
    wprep = k.sb("wprep", [128, D], F32)
    wstg = [k.sb(f"wstg{i}", [128, D], BF16) for i in range(2)]
    wsl = [k.sb(f"wsl{i}", [128, D], BF16) for i in range(4)]
    rwkv_t = [k.sb(f"rwt{c}", [128, TT], BF16) for c in range(NC)]
    zrow = [k.sb(f"zrow{c}", [128, TT], F32) for c in range(NC)]
    xres = [k.sb(f"xres{i}", [128, TT], F32) for i in range(2)]
    x1tb = [k.sb(f"x1tb{i}", [128, TT], BF16) for i in range(2)]
    ln05_col = k.sb("ln05_col", [128, 1], F32)

    pj = [k.ps(f"pj{i}", [128, TT], F32) for i in range(8)]
    ln = PostLN(k, "ln1", dict(sum=pj[2][0:1, :], sq=pj[3][0:1, :],
                               b1=pj[4][:], b2=pj[5][:]))

    s_ld = k.sem("ld")
    s_st = k.sem("st")
    s_mm = k.sem("mm")
    s_ev = k.sem("ev")
    s_wk = k.sem("wk")
    s_ms = k.sem("ms")

    e_tm = {p: k.dma("sync", tm_all[p][:], io["tm"][p], inc=s_ld)
            for p in ("k", "v", "r")}
    e_td = k.dma("sync", ew_all[:], io["tdec"], inc=s_ld)
    e_tf = k.dma("sync", u_all[:], io["tfir"], inc=s_ld)
    e_ew = k.op("scalar", lambda e: e.activation(w_ek[:, 0:NC], ew_all[:], AF.Exp),
                wait=[e_td], inc=s_ms)
    e_ew = k.op("scalar", lambda e: e.activation(ew_all[:], w_ek[:, 0:NC], AF.Exp,
                                                 scale=-1.0),
                wait=[e_ew], inc=s_ms)
    e_1m = {p: k.op("vector", lambda e, p=p: e.tensor_scalar(
        tm1m[p][:], tm_all[p][:], -1.0, 1.0, ALU.mult, ALU.add),
        wait=[e_tm[p]], inc=s_ms) for p in ("k", "v", "r")}
    e_l5 = k.op("vector", lambda e: e.memset(ln05_col[:], math.log(0.5)), inc=s_ms)

    e_xbf = []
    for c in range(NC):
        ez = k.op("vector", lambda e, c=c: e.memset(x_bf[c][:, 0:1], 0.0), inc=s_ms)
        e_xbf.append(k.dma("gpsimd", x_bf[c][:, 1:T + 1],
                           io["x_cm"][c * 128:(c + 1) * 128, :],
                           wait=[ez], inc=s_ld))

    e_sc_store = {}
    e_prev_stg = [None, None]
    e_prev_prep = None
    for p in ("k", "v", "r"):
        for c in range(NC):
            e_in = k.dma("sync", wprep[:], io["wT"][p][c * 128:(c + 1) * 128, :],
                         wait=[e_prev_prep], inc=s_ld)
            ea = k.op("vector", lambda e, p=p, c=c: e.tensor_scalar_mul(
                wstg[0][:], wprep[:], tm_all[p][:, c:c + 1]),
                wait=[e_in, e_tm[p], e_prev_stg[0]], inc=s_ev)
            eb = k.op("vector", lambda e, p=p, c=c: e.tensor_scalar_mul(
                wstg[1][:], wprep[:], tm1m[p][:, c:c + 1]),
                wait=[e_in, e_1m[p], e_prev_stg[1]], inc=s_ev)
            e_prev_prep = eb
            outs = []
            for abi, ab in enumerate(("a", "b")):
                dst = io["wsc5"][p + ab].rearrange("(d prt c j) -> prt d c j",
                                                   d=NC, prt=128, c=NC)[:, :, c, :]
                outs.append(k.dma("sync", dst,
                                  wstg[abi][:].rearrange("p (d j) -> p d j", d=NC),
                                  wait=[ea if abi == 0 else eb], inc=s_st))
            e_prev_stg = outs
            e_sc_store[(p, c)] = outs
    e_wo_store = {}
    for c in range(NC):
        e_in = k.dma("gpsimd", wstg[0][:], io["woT"][c * 128:(c + 1) * 128, :],
                     wait=[e_prev_stg[0], e_prev_prep], inc=s_ld)
        dst = io["wo5"].rearrange("(d prt c j) -> prt d c j",
                                  d=NC, prt=128, c=NC)[:, :, c, :]
        eo = k.dma("sync", dst, wstg[0][:].rearrange("p (d j) -> p d j", d=NC),
                   wait=[e_in], inc=s_st)
        e_prev_stg[0] = eo
        e_prev_prep = None
        e_wo_store[c] = eo
    e_wo_all = [e_wo_store[c] for c in range(NC)]

    bank_free = [None] * 8
    free_k = [None] * NQ
    free_v = [None] * NQ
    free_sr = [None] * NQ
    free_rw = None
    e_slab_use = [None] * 4
    kv_ready = {}
    sr_ready = {}
    e_rwkv_store = [None] * NC

    for d in range(NC):
        kv_ready = {q: [] for q in range(NQ)}
        sr_ready = {q: [] for q in range(NQ)}
        for pi, p in enumerate(("k", "v", "r")):
            for half in range(2):
                tset = list(range(half * 4, half * 4 + 4))
                e_mm = {t: None for t in tset}
                for abi, ab in enumerate(("a", "b")):
                    sl = wsl[2 * half + abi]
                    src = io["wsc5"][p + ab][d * 128 * D:(d + 1) * 128 * D]
                    src = src.rearrange("(prt c j) -> prt (c j)", prt=128, c=NC)
                    e_w = k.dma("sync", sl[:], src,
                                wait=[e_slab_use[2 * half + abi],
                                      [e_sc_store[(p, c)][abi] for c in range(NC)]],
                                inc=s_ld)
                    for c in range(NC):
                        lhs = sl[:, c * 128:(c + 1) * 128]
                        for t in tset:
                            off = 1 - abi
                            rhs = x_bf[c][:, off + t * TT: off + t * TT + TT]
                            first = (abi == 0 and c == 0)
                            last = (abi == 1 and c == NC - 1)
                            waits = [e_w, e_xbf[c]]
                            if first:
                                waits.append(bank_free[t])
                                q = (t * TT) // QT
                                if p == "k":
                                    waits.append(free_k[q])
                                elif p == "v":
                                    waits.append(free_v[q])
                                else:
                                    waits.append(free_sr[q])
                            e_mm[t] = k.op("tensor",
                                           lambda e, lhs=lhs, rhs=rhs, b=pj[t],
                                           first=first, last=last:
                                           e.matmul(b[:], lhs, rhs,
                                                    start=first, stop=last),
                                           wait=waits, inc=s_mm)
                    e_slab_use[2 * half + abi] = e_mm[tset[-1]]
                for t in tset:
                    q = (t * TT) // QT
                    if p == "r":
                        ee = k.op("scalar", lambda e, t=t: e.activation(
                            srsb[:, t * TT:(t + 1) * TT], pj[t][:], AF.Tanh,
                            scale=0.5),
                            wait=[e_mm[t]], inc=s_ev)
                        sr_ready[q].append(ee)
                    else:
                        dst = ksb if p == "k" else vsb
                        ee = k.op("vector", lambda e, t=t, dst=dst: e.tensor_copy(
                            dst[:, t * TT:(t + 1) * TT], pj[t][:]),
                            wait=[e_mm[t]], inc=s_ev)
                        kv_ready[q].append(ee)
                    bank_free[t] = ee

        ew_b = ew_all[:, d:d + 1].broadcast_to([128, QT])
        u_col = u_all[:, d:d + 1]
        e_seg_last = None
        for q in range(NQ):
            t0 = q * QT
            kk = ksb[:, t0:t0 + QT]
            vv = vsb[:, t0:t0 + QT]
            sr = srsb[:, t0:t0 + QT]
            initA = 0.0 if q == 0 else stA[:, 0:1]
            initB = 0.0 if q == 0 else stB[:, 0:1]
            if q == 0:
                k.op("vector", lambda e: e.memset(w_A[:, 0:1], 0.0), inc=s_wk)
                k.op("vector", lambda e: e.memset(w_B[:, 0:1], 0.0), inc=s_wk)
            else:
                k.op("vector", lambda e: e.tensor_copy(w_A[:, 0:1], stA[:]),
                     inc=s_wk)
                k.op("vector", lambda e: e.tensor_copy(w_B[:, 0:1], stB[:]),
                     inc=s_wk)
            e1 = k.op("scalar", lambda e, kk=kk: e.activation(w_ek[:], kk, AF.Exp),
                      wait=[kv_ready[q], e_seg_last], inc=s_wk)
            e2 = k.op("vector", lambda e, vv=vv: e.tensor_mul(w_ekv[:], w_ek[:], vv),
                      wait=[e1, kv_ready[q]], inc=s_wk)
            e3 = k.op("vector", lambda e, initA=initA, ew_b=ew_b: e.tensor_tensor_scan(
                w_A[:, 1:QT + 1], ew_b, w_ekv[:], initA, ALU.mult, ALU.add),
                wait=[e2, e_ew], inc=s_wk)
            e4 = k.op("vector", lambda e, initB=initB, ew_b=ew_b: e.tensor_tensor_scan(
                w_B[:, 1:QT + 1], ew_b, w_ek[:], initB, ALU.mult, ALU.add),
                wait=[e1], inc=s_wk)
            if dbg is not None and d == 0 and q == 0:
                eda = k.dma("gpsimd", dbg["wA"], w_A[:], wait=[e3], inc=s_st)
                edb = k.dma("gpsimd", dbg["wB"], w_B[:], wait=[e4], inc=s_st)
                k.op("vector", lambda e: e.nop(), wait=[eda, edb], inc=s_wk)
            e5 = k.op("vector", lambda e: e.tensor_copy(stA[:], w_A[:, QT:QT + 1]),
                      inc=s_wk)
            e6 = k.op("vector", lambda e: e.tensor_copy(stB[:], w_B[:, QT:QT + 1]),
                      inc=s_wk)
            e7 = k.op("scalar", lambda e, kk=kk, u_col=u_col: e.activation(
                w_ek[:], kk, AF.Exp, bias=u_col), wait=[e4, e_tf], inc=s_wk)
            free_k[q] = e7
            e8 = k.op("vector", lambda e, vv=vv: e.tensor_mul(w_n[:], w_ek[:], vv),
                      wait=[e7], inc=s_wk)
            free_v[q] = e8
            e9 = k.op("vector", lambda e: e.tensor_add(w_n[:], w_n[:], w_A[:, 0:QT]),
                      inc=s_wk)
            e10 = k.op("vector", lambda e: e.tensor_add(w_d2[:], w_ek[:], w_B[:, 0:QT]),
                       wait=[e7], inc=s_wk)
            e11 = k.op("scalar", lambda e: e.activation(w_ekv[:], w_d2[:], AF.Ln),
                       wait=[e10, e3], inc=s_wk)
            e12 = k.op("scalar", lambda e: e.activation(
                w_d2[:], w_ekv[:], AF.Exp, scale=-1.0, bias=ln05_col[:, 0:1]),
                wait=[e11, e_l5], inc=s_wk)
            e13 = k.op("vector", lambda e: e.tensor_mul(w_n[:], w_n[:], w_d2[:]),
                       wait=[e9, e12], inc=s_wk)
            e14 = k.op("vector", lambda e, sr=sr, t0=t0: e.scalar_tensor_tensor(
                rwkv_bf[:, t0:t0 + QT], sr, 1.0, w_n[:], ALU.add, ALU.mult),
                wait=[e13, sr_ready[q], free_rw], inc=s_wk)
            free_sr[q] = e14
            e_seg_last = e14
        free_rw = k.dma("sync", io["rwkv_d"][d * 128:(d + 1) * 128, :], rwkv_bf[:],
                        wait=[e_seg_last], inc=s_st)
        e_rwkv_store[d] = free_rw
        if dbg is not None and d == 0:
            k.dma("gpsimd", dbg["ew"], ew_all[:], wait=[e_seg_last], inc=s_st)
            k.dma("gpsimd", dbg["k0"], ksb[:], wait=[e_seg_last], inc=s_st)
            k.dma("gpsimd", dbg["v0"], vsb[:], wait=[e_seg_last], inc=s_st)
            k.dma("gpsimd", dbg["sr0"], srsb[:], wait=[e_seg_last], inc=s_st)

    e_bar = k.barrier(["tensor", "vector", "scalar"])

    def prev2(d_, t_):
        return (t_, d_ - 2) if d_ >= 2 else (t_ - 1, d_ + NC - 2)

    e_rw_use = {}
    e_slab_use4 = [e_bar, e_bar]
    e_stats_read = None
    e_zadd = {}
    e_x1_dma = {}
    for t in range(NT):
        e_rw = {}
        for c in range(NC):
            e_rw[c] = k.dma("scalar", rwkv_t[c][:],
                            io["rwkv_d"][c * 128:(c + 1) * 128, t * TT:(t + 1) * TT],
                            wait=[e_rwkv_store[c], e_rw_use.get(c), e_bar],
                            inc=s_ld)
        e_sum_mm = None
        e_sq_mm = None
        for dout in range(NC):
            bank = pj[dout % 2]
            sl = wsl[dout % 2]
            src = io["wo5"][dout * 128 * D:(dout + 1) * 128 * D]
            src = src.rearrange("(prt c j) -> prt (c j)", prt=128, c=NC)
            e_w = k.dma("sync", sl[:], src,
                        wait=[e_wo_all, e_slab_use4[dout % 2]], inc=s_ld)
            e_mm = None
            for c in range(NC):
                e_mm = k.op("tensor", lambda e, sl=sl, c=c, bank=bank:
                            e.matmul(bank[:], sl[:, c * 128:(c + 1) * 128],
                                     rwkv_t[c][:],
                                     start=(c == 0), stop=(c == NC - 1)),
                            wait=[e_w, e_rw[c]] +
                                 ([e_zadd.get(prev2(dout, t))] if c == 0 else []),
                            inc=s_mm)
                e_rw_use[c] = e_mm
            e_slab_use4[dout % 2] = e_mm
            xr = xres[dout % 2]
            e_x = k.dma("scalar", xr[:],
                        io["x_cm"][dout * 128:(dout + 1) * 128, t * TT:(t + 1) * TT],
                        wait=[e_zadd.get(prev2(dout, t)), e_bar], inc=s_ld)
            e_z = k.op("vector", lambda e, dout=dout, bank=bank, xr=xr: e.tensor_add(
                zrow[dout][:], bank[:], xr[:]),
                wait=[e_mm, e_x, e_stats_read], inc=s_ev)
            e_zadd[(t, dout)] = e_z
        def sink(dout, xt, ev, t=t):
            xtb = x1tb[dout % 2]
            e_n3 = k.op("vector", lambda e, xt=xt, xtb=xtb: e.tensor_copy(
                xtb[:], xt[:]),
                wait=[e_x1_dma.get(prev2(dout, t) + (1,))])
            o1 = k.dma("sync", io["x1f"][dout * 128:(dout + 1) * 128,
                                         t * TT:(t + 1) * TT],
                       xt[:], wait=[ev])
            o2 = k.dma("sync", io["x1b"][dout * 128:(dout + 1) * 128,
                                         t * TT:(t + 1) * TT],
                       xtb[:], wait=[e_n3])
            e_x1_dma[(t, dout, 0)] = o1
            e_x1_dma[(t, dout, 1)] = o2
            return [o1, e_n3]

        e_stats_read, _ = ln.tile([zrow[d_][:] for d_ in range(NC)],
                                  [e_zadd[(t, d_)] for d_ in range(NC)], sink)

    k.op("vector", lambda e: e.engine_nop(),
         wait=[k.queue_done("sync"), k.queue_done("gpsimd"),
               k.queue_done("scalar")])


def build_A1(dbg=False):
    nc = bass.Bass("TRN2", target_bir_lowering=False)
    io = {}
    if dbg:
        io["dbg"] = {
            "k0": nc.dram_tensor("dbg_k0", [128, T], F32, kind="ExternalOutput").ap(),
            "ew": nc.dram_tensor("dbg_ew", [128, NC], F32, kind="ExternalOutput").ap(),
            "wA": nc.dram_tensor("dbg_wA", [128, QT + 1], F32, kind="ExternalOutput").ap(),
            "wB": nc.dram_tensor("dbg_wB", [128, QT + 1], F32, kind="ExternalOutput").ap(),
            "v0": nc.dram_tensor("dbg_v0", [128, T], F32, kind="ExternalOutput").ap(),
            "sr0": nc.dram_tensor("dbg_sr0", [128, T], F32, kind="ExternalOutput").ap(),
        }
    io["x_cm"] = nc.dram_tensor("x_cm", [D, T], F32, kind="ExternalInput").ap()
    io["wT"] = {p: nc.dram_tensor(f"w{p}T", [D, D], F32, kind="ExternalInput").ap()
                for p in ("k", "v", "r")}
    io["woT"] = nc.dram_tensor("woT", [D, D], F32, kind="ExternalInput").ap()
    for nm in ("tmk", "tmv", "tmr"):
        pass
    io["tm"] = {p: nc.dram_tensor(f"tm{p}", [128, NC], F32, kind="ExternalInput").ap()
                for p in ("k", "v", "r")}
    io["tdec"] = nc.dram_tensor("tdec", [128, NC], F32, kind="ExternalInput").ap()
    io["tfir"] = nc.dram_tensor("tfir", [128, NC], F32, kind="ExternalInput").ap()
    io["x1f"] = nc.dram_tensor("x1f", [D, T], F32, kind="ExternalOutput").ap()
    io["x1b"] = nc.dram_tensor("x1b", [D, T], BF16, kind="ExternalOutput").ap()
    io["wsc5"] = {p + ab: nc.dram_tensor(f"wsc_{p}{ab}", [NC * 128 * NC * 128], BF16,
                                         kind="Internal").ap()
                  for p in ("k", "v", "r") for ab in ("a", "b")}
    io["wo5"] = nc.dram_tensor("wo5", [NC * 128 * NC * 128], BF16, kind="Internal").ap()
    io["rwkv_d"] = nc.dram_tensor("rwkv_d", [D, T], BF16,
                                  kind="ExternalOutput" if dbg else "Internal").ap()

    k = KB(nc)
    emit_A1(k, io)
    k.build()
    return nc



NE = 15
BLK = 1024
NBLK = T // BLK
NSUB = BLK // TT


def emit_A2(k, io):
    nc = k.nc
    dbg = io.get("dbg")
    dbg_evs = []

    x1blk = [[k.sb(f"x1b{i}_{c}", [128, BLK], BF16) for c in range(NC)]
             for i in range(2)]
    w1b = [[k.sb(f"w1b{i}_{c}", [128, D], BF16) for c in range(NC)]
           for i in range(2)]
    w2b = [[k.sb(f"w2b{i}_{c}", [128, D], BF16) for c in range(NC)]
           for i in range(2)]
    hsc = [k.sb(f"hsc{h}", [128, BLK], BF16) for h in range(NC)]
    routed = [k.sb(f"rout{d}", [128, BLK], F32) for d in range(NC)]
    tmp = [k.sb(f"tmpe{i}", [128, TT], F32) for i in range(2)]
    cw_b = k.sb("cw_b", [128, TT], F32)
    xres = [k.sb(f"xres{i}", [128, TT], F32) for i in range(2)]
    x2tb = [k.sb(f"x2tb{i}", [128, TT], BF16) for i in range(2)]
    rwb = k.sb("rwb", [128, NE * NC], BF16)
    rbb = k.sb("rbb", [128, 16], F32)
    rb_row = k.sb("rb_row", [1, NE], F32)
    lg = k.sb("lg", [128, 16], F32)
    exs = k.sb("exs", [128, 16], F32)
    nmx = k.sb("nmx", [128, 1], F32)
    ssum = k.sb("ssum", [128, 1], F32)
    ssum2 = k.sb("ssum2", [128, 1], F32)
    probs = k.sb("probs", [128, 16], F32)
    mw8 = k.sb("mw8", [128, 8], F32)
    maskf = k.sb("maskf", [128, 16], F32)
    cwf = k.sb("cwf", [128, 16], F32)
    cw_pad = k.sb("cw_pad", [128, 128], BF16)
    ctp = k.sb("ctp", [128, 128], BF16)
    cwT = k.sb("cwT", [15, BLK], BF16)
    sb1p = k.sb("sb1t", [128, NC], F32)
    sb2p = k.sb("sb2t", [128, NC], F32)
    eb1p = k.sb("eb1t", [128, NE * NC], F32)
    eb2b = k.sb("eb2b", [15, D], BF16)
    ohb = k.sb("ohb", [15, NE * 128], BF16)

    pj = [k.ps(f"pj{i}", [128, TT], F32) for i in range(8)]
    ln = PostLN(k, "ln2", dict(sum=pj[5][0:1, :], sq=pj[6][0:1, :],
                               b1=pj[7][:], b2=pj[7][:]))
    lgps = pj[4][:, 0:16]

    e_rw = []
    for c in range(NC):
        e_rw.append(k.dma("gpsimd", rwb[:, c * NE:(c + 1) * NE],
                          io["router_W"][c * 128:(c + 1) * 128, :]))
    e_rbrow = k.dma("sync", rb_row[:], io["router_b"])
    e_sb1 = k.dma("sync", sb1p[:], io["sb1p"])
    e_sb2 = k.dma("sync", sb2p[:], io["sb2p"])
    e_eb1 = k.dma("sync", eb1p[:], io["eb1p"])
    e_eb2 = k.dma("gpsimd", eb2b[:], io["eb2"])
    e_oh = k.dma("sync", ohb[:], io["oh"])
    e_cwz = k.op("vector", lambda e: e.memset(cw_pad[:], 0.0))
    e_rbb = k.op("tensor", lambda e: e.matmul(lgps[:, 0:NE], ln.ones_row[:],
                                              rb_row[:], start=True, stop=True),
                 wait=[e_rbrow, ln.e_or])
    e_rbb = k.op("vector", lambda e: e.tensor_copy(rbb[:, 0:NE], lgps[:, 0:NE]),
                 wait=[e_rbb])
    bank4_free = e_rbb

    e_xblk_use = [None, None]
    e_wuse = [None, None]
    e_l2last = {}
    e_relu = {}
    e_tmp_prev = [None, None]
    e_bankL1 = [None, None]
    e_bankL2 = [None, None]
    e_routed_war = None
    e_x2_dma = {}
    e_xres_prev = [None, None]
    e_cwT_ready = None
    e_cwT_read = None
    e_cwb_read = None
    e_ctp_use = None

    def prev2k(d_, blk_):
        return (blk_, d_ - 2) if d_ >= 2 else (blk_ - 1, d_ + NC - 2)

    for blk in range(NBLK):
        bpar = blk % 2
        xb = x1blk[bpar]
        e_xb = []
        for c in range(NC):
            e_xb.append(k.dma("sync", xb[c][:],
                              io["x1b"][c * 128:(c + 1) * 128,
                                        blk * BLK:(blk + 1) * BLK],
                              wait=[e_xblk_use[bpar]]))
        e_cwT = []
        e3_prev = None
        e_e12_prev = None
        for ts in range(NC):
            e_lg = None
            for c in range(NC):
                e_lg = k.op("tensor", lambda e, c=c, ts=ts, xb=xb: e.matmul(
                    lgps[:, 0:NE], xb[c][:, ts * 128:(ts + 1) * 128],
                    rwb[:, c * NE:(c + 1) * NE],
                    start=(c == 0), stop=(c == NC - 1)),
                    wait=[e_xb[c], e_rw[c]] + ([bank4_free] if c == 0 else []) +
                         (dbg_evs if dbg is not None and blk == 0 and ts == 1
                          and c == 0 else []))
            e1 = k.op("vector", lambda e: e.tensor_add(lg[:, 0:NE], lgps[:, 0:NE],
                                                       rbb[:, 0:NE]),
                      wait=[e_lg, e_rbb, e3_prev])
            bank4_free = e1
            e1b = k.op("vector", lambda e: e.memset(lg[:, NE:16], -1e38), wait=[])
            e2 = k.op("vector", lambda e: e.reduce_max(nmx[:], lg[:], axis=AX.X),
                      wait=[e3_prev], strict=True, fence=True)
            e2b = k.op("vector", lambda e: e.tensor_scalar_mul(nmx[:], nmx[:], -1.0),
                      wait=[e2])
            e3 = k.op("scalar", lambda e: e.activation(exs[:], lg[:], AF.Exp,
                                                       bias=nmx[:, 0:1]),
                      wait=[e2b, e1])
            e4 = k.op("vector", lambda e: e.reduce_sum(ssum[:], exs[:, 0:NE],
                                                       axis=AX.X), wait=[e3],
                      strict=True, fence=True)
            e3_prev = e3
            e5a = k.op("scalar", lambda e: e.activation(ssum2[:], ssum[:], AF.Ln),
                       wait=[e4])
            e5 = k.op("scalar", lambda e: e.activation(ssum2[:], ssum2[:], AF.Exp,
                                                       scale=-1.0))
            e6 = k.op("vector", lambda e: e.tensor_scalar_mul(probs[:, 0:NE],
                                                              exs[:, 0:NE],
                                                              ssum2[:, 0:1]),
                      wait=[e3, e5] + (dbg_evs if dbg is not None and blk == 0
                                       and ts == 1 else []))
            e6b = k.op("vector", lambda e: e.memset(probs[:, NE:16], -1e38))
            e7 = k.op("vector", lambda e: e.max(mw8[:], probs[:]),
                      wait=[e6b], strict=True, fence=True)
            e8 = k.op("vector", lambda e: e.tensor_single_scalar(
                maskf[:, 0:NE], probs[:, 0:NE], mw8[:, 2:3], ALU.is_ge),
                wait=[e7], strict=True, fence=True)
            e9 = k.op("vector", lambda e: e.tensor_mul(cwf[:, 0:NE],
                                                       maskf[:, 0:NE],
                                                       probs[:, 0:NE]),
                      wait=[e8], strict=True, fence=True)
            if dbg is not None and blk == 0 and ts == 0:
                dbg_gate = [k.dma("gpsimd", dbg["probs0"], probs[:], wait=[e9]),
                            k.dma("gpsimd", dbg["mw80"], mw8[:], wait=[e9]),
                            k.dma("gpsimd", dbg["lg0"], lg[:], wait=[e9]),
                            k.dma("gpsimd", dbg["rbb0"], rbb[:], wait=[e9]),
                            k.dma("gpsimd", dbg["mask0"], cwf[:], wait=[e9])]
                dbg_evs.extend(dbg_gate)
            elif dbg is not None and blk == 0 and ts == 1:
                pass
            e10 = k.op("vector", lambda e, ts=ts: e.tensor_copy(
                cw_pad[:, 0:NE], cwf[:, 0:NE]),
                wait=[e_cwz, e_ctp_use], strict=True, fence=True)
            e11 = k.dma("scalar", ctp[:], cw_pad[:], transpose=True,
                        wait=[e10, e_e12_prev])
            e_ctp_use = e11
            e12 = k.op("vector", lambda e, ts=ts: e.tensor_copy(
                cwT[:, ts * 128:ts * 128 + 128], ctp[0:NE, :]),
                wait=[e11, e_cwT_read])
            e_e12_prev = e12
            e_cwT.append(e12)
        e_cwT_all = list(e_cwT)
        if dbg is not None and blk == 0:
            dbg_evs.append(k.dma("gpsimd", dbg["cwT0"], cwT[:],
                                 wait=[e_cwT_all]))

        for ei in range(-1, NE):
            wpar = (ei + 1) % 2
            w1 = w1b[wpar]
            w2 = w2b[wpar]
            e_w1 = []
            e_w2 = []
            if ei < 0:
                for c in range(NC):
                    e_w1.append(k.dma("gpsimd", w1[c][:],
                                      io["sW1"][c * 128:(c + 1) * 128, :],
                                      wait=[e_wuse[wpar]]))
                    e_w2.append(k.dma("gpsimd", w2[c][:],
                                      io["sW2"][c * 128:(c + 1) * 128, :],
                                      wait=[e_wuse[wpar]]))
            else:
                for c in range(NC):
                    e_w1.append(k.dma("gpsimd", w1[c][:],
                                      io["eW1"][(ei * D + c * 128) * D:
                                                (ei * D + c * 128 + 128) * D]
                                      .rearrange("(p j) -> p j", p=128),
                                      wait=[e_wuse[wpar]]))
                    e_w2.append(k.dma("gpsimd", w2[c][:],
                                      io["eW2"][(ei * D + c * 128) * D:
                                                (ei * D + c * 128 + 128) * D]
                                      .rearrange("(p j) -> p j", p=128),
                                      wait=[e_wuse[wpar]]))
            for st in range(NSUB):
                tsl = slice(st * TT, (st + 1) * TT)
                if ei >= 0:
                    e_cb = k.op("tensor", lambda e, ei=ei, st=st: e.matmul(
                        pj[4][:], ohb[:, ei * 128:(ei + 1) * 128],
                        cwT[:, st * TT:(st + 1) * TT],
                        start=True, stop=True),
                        wait=[e_cwT_all, e_oh, bank4_free])
                    e_cwT_read = e_cb
                    e_cbc = k.op("vector", lambda e: e.tensor_copy(cw_b[:], pj[4][:]),
                                 wait=[e_cb, e_cwb_read])
                    bank4_free = e_cbc
                for hd in range(NC):
                    bank = pj[hd % 2]
                    e_mm = None
                    for c in range(NC):
                        e_mm = k.op("tensor", lambda e, c=c, hd=hd, xb=xb, w1=w1,
                                    bank=bank, tsl=tsl: e.matmul(
                                        bank[:], w1[c][:, hd * 128:(hd + 1) * 128],
                                        xb[c][:, tsl], start=(c == 0),
                                        stop=(c == NC - 1)),
                                    wait=[e_w1[c], e_xb[c]] +
                                         ([e_bankL1[hd % 2]] if c == 0 else []))
                    if ei < 0:
                        e_ev = k.op("scalar", lambda e, hd=hd, bank=bank, st=st:
                                    e.activation(hsc[hd][:, st * TT:(st + 1) * TT],
                                                 bank[:], AF.Relu,
                                                 bias=sb1p[:, hd:hd + 1]),
                                    wait=[e_mm, e_sb1, e_l2last.get(st)])
                    else:
                        e_st = k.op("vector", lambda e, hd=hd, bank=bank, ei=ei,
                                    b=tmp[hd % 2]: e.scalar_tensor_tensor(
                                        b[:], bank[:],
                                        eb1p[:, ei * NC + hd:ei * NC + hd + 1],
                                        cw_b[:], ALU.add, ALU.mult),
                                    wait=[e_mm, e_eb1, e_cbc,
                                          e_tmp_prev[hd % 2]])
                        e_cwb_read = e_st
                        e_ev = k.op("scalar", lambda e, hd=hd, st=st, b=tmp[hd % 2]:
                                    e.activation(hsc[hd][:, st * TT:(st + 1) * TT],
                                                 b[:], AF.Relu),
                                    wait=[e_st, e_l2last.get(st)])
                        e_tmp_prev[hd % 2] = e_ev
                    e_relu[(st, hd)] = e_ev
                    e_bankL1[hd % 2] = e_ev
                for d2 in range(NC):
                    bank = pj[2 + d2 % 2]
                    e_mm = None
                    for hd in range(NC):
                        e_mm = k.op("tensor", lambda e, hd=hd, d2=d2, w2=w2,
                                    bank=bank, st=st: e.matmul(
                                        bank[:], w2[hd][:, d2 * 128:(d2 + 1) * 128],
                                        hsc[hd][:, st * TT:(st + 1) * TT],
                                        start=(hd == 0), stop=(hd == NC - 1)),
                                    wait=[e_w2[hd], e_relu[(st, hd)]] +
                                         ([e_bankL2[d2 % 2]] if hd == 0 else []))
                    e_l2last[st] = e_mm
                    if ei < 0:
                        e_ac = k.op("vector", lambda e, d2=d2, bank=bank, tsl=tsl:
                                    e.tensor_scalar(routed[d2][:, tsl], bank[:],
                                                    sb2p[:, d2:d2 + 1], None,
                                                    ALU.add),
                                    wait=[e_mm, e_sb2, e_routed_war])
                    else:
                        e_ac = k.op("vector", lambda e, d2=d2, bank=bank, tsl=tsl:
                                    e.tensor_add(routed[d2][:, tsl],
                                                 routed[d2][:, tsl], bank[:]),
                                    wait=[e_mm])
                    e_bankL2[d2 % 2] = e_ac
            e_wuse[wpar] = e_l2last[NSUB - 1]
        e_xblk_use[bpar] = e_l2last[NSUB - 1]

        for st in range(NSUB):
            tsl = slice(st * TT, (st + 1) * TT)
            e_zadds = []
            for d2 in range(NC):
                e_eb = k.op("tensor", lambda e, d2=d2, st=st: e.matmul(
                    pj[4][:], eb2b[:, d2 * 128:(d2 + 1) * 128],
                    cwT[:, st * TT:(st + 1) * TT], start=True, stop=True),
                    wait=[e_eb2, e_cwT_all, bank4_free])
                e_cwT_read = e_eb
                e_ad = k.op("vector", lambda e, d2=d2, tsl=tsl: e.tensor_add(
                    routed[d2][:, tsl], routed[d2][:, tsl], pj[4][:]),
                    wait=[e_eb])
                bank4_free = e_ad
                xr = xres[d2 % 2]
                e_x = k.dma("scalar", xr[:],
                            io["x1f"][d2 * 128:(d2 + 1) * 128,
                                      blk * BLK + st * TT:blk * BLK + (st + 1) * TT],
                            wait=[e_xres_prev[d2 % 2]])
                e_z = k.op("vector", lambda e, d2=d2, tsl=tsl, xr=xr: e.tensor_add(
                    routed[d2][:, tsl], routed[d2][:, tsl], xr[:]),
                    wait=[e_x])
                e_xres_prev[d2 % 2] = e_z
                e_zadds.append(e_z)

            tglob = blk * NSUB + st
            if dbg is not None and tglob == 0:
                for d2 in range(NC):
                    dbg_evs.append(k.dma("gpsimd",
                                         dbg["z0"][d2 * 128:(d2 + 1) * 128, :],
                                         routed[d2][:, 0:TT], wait=[e_zadds[d2]]))

            def sink(dout, xt, ev, tglob=tglob):
                xtb = x2tb[dout % 2]
                e_n3 = k.op("vector", lambda e, xt=xt, xtb=xtb: e.tensor_copy(
                    xtb[:], xt[:]),
                    wait=[e_x2_dma.get((tglob - (1 if dout < 2 else 0),
                                        (dout - 2) % NC, 1))])
                o1 = k.dma("sync", io["x2f"][dout * 128:(dout + 1) * 128,
                                             tglob * TT:(tglob + 1) * TT],
                           xt[:], wait=[ev])
                o2 = k.dma("sync", io["x2b"][dout * 128:(dout + 1) * 128,
                                             tglob * TT:(tglob + 1) * TT],
                           xtb[:], wait=[e_n3])
                e_x2_dma[(tglob, dout, 0)] = o1
                e_x2_dma[(tglob, dout, 1)] = o2
                return [o1, e_n3]

            e_sr, _ = ln.tile([routed[d_][:, tsl] for d_ in range(NC)],
                              e_zadds, sink)
            e_routed_war = e_sr

    k.op("vector", lambda e: e.engine_nop(),
         wait=[k.queue_done("sync"), k.queue_done("gpsimd"),
               k.queue_done("scalar")])


def build_A2(dbg=False):
    nc = bass.Bass("TRN2", target_bir_lowering=False)
    io = {}
    if dbg:
        io["dbg"] = {
            "cwT0": nc.dram_tensor("dbg_cwT0", [NE, BLK], BF16,
                                   kind="ExternalOutput").ap(),
            "z0": nc.dram_tensor("dbg_z0", [D, TT], F32,
                                 kind="ExternalOutput").ap(),
            "probs0": nc.dram_tensor("dbg_probs0", [128, 16], F32,
                                     kind="ExternalOutput").ap(),
            "mw80": nc.dram_tensor("dbg_mw80", [128, 8], F32,
                                   kind="ExternalOutput").ap(),
            "mask0": nc.dram_tensor("dbg_mask0", [128, 16], F32,
                                    kind="ExternalOutput").ap(),
            "lg0": nc.dram_tensor("dbg_lg0", [128, 16], F32,
                                  kind="ExternalOutput").ap(),
            "rbb0": nc.dram_tensor("dbg_rbb0", [128, 16], F32,
                                   kind="ExternalOutput").ap(),
        }
    io["x1f"] = nc.dram_tensor("x1f", [D, T], F32, kind="ExternalInput").ap()
    io["x1b"] = nc.dram_tensor("x1b", [D, T], BF16, kind="ExternalInput").ap()
    io["router_W"] = nc.dram_tensor("router_W", [D, NE], F32, kind="ExternalInput").ap()
    io["router_b"] = nc.dram_tensor("router_b", [1, NE], F32, kind="ExternalInput").ap()
    io["sW1"] = nc.dram_tensor("sW1", [D, D], F32, kind="ExternalInput").ap()
    io["sW2"] = nc.dram_tensor("sW2", [D, D], F32, kind="ExternalInput").ap()
    io["eW1"] = nc.dram_tensor("eW1", [NE * D * D], F32, kind="ExternalInput").ap()
    io["eW2"] = nc.dram_tensor("eW2", [NE * D * D], F32, kind="ExternalInput").ap()
    io["sb1p"] = nc.dram_tensor("sb1p", [128, NC], F32, kind="ExternalInput").ap()
    io["sb2p"] = nc.dram_tensor("sb2p", [128, NC], F32, kind="ExternalInput").ap()
    io["eb1p"] = nc.dram_tensor("eb1p", [128, NE * NC], F32, kind="ExternalInput").ap()
    io["eb2"] = nc.dram_tensor("eb2", [NE, D], F32, kind="ExternalInput").ap()
    io["oh"] = nc.dram_tensor("oh", [NE, NE * 128], BF16, kind="ExternalInput").ap()
    io["x2f"] = nc.dram_tensor("x2f", [D, T], F32, kind="ExternalOutput").ap()
    io["x2b"] = nc.dram_tensor("x2b", [D, T], BF16, kind="ExternalOutput").ap()
    k = KB(nc)
    emit_A2(k, io)
    k.build()
    return nc


def prep_A2_inputs(inputs, x1f, x1b):
    oh = np.zeros((NE, NE * 128), dtype=BF)
    for e in range(NE):
        oh[e, e * 128:(e + 1) * 128] = 1.0
    return {
        "x1f": x1f, "x1b": x1b,
        "router_W": inputs["L0_ffn_router_W"],
        "router_b": inputs["L0_ffn_router_b"].reshape(1, NE),
        "sW1": inputs["L0_ffn_sW1"], "sW2": inputs["L0_ffn_sW2"],
        "eW1": inputs["L0_ffn_eW1"].reshape(-1),
        "eW2": inputs["L0_ffn_eW2"].reshape(-1),
        "sb1p": _plane(inputs["L0_ffn_sb1"]),
        "sb2p": _plane(inputs["L0_ffn_sb2"]),
        "eb1p": np.ascontiguousarray(
            inputs["L0_ffn_eb1"].reshape(NE, NC, 128).transpose(2, 0, 1)
            .reshape(128, NE * NC)),
        "eb2": inputs["L0_ffn_eb2"],
        "oh": oh,
    }



T2 = 256
NT2 = T // T2


def emit_A3(k, io):
    nc = k.nc

    x2bf = [k.sb(f"x2bf{c}", [128, T], BF16) for c in range(NC)]
    xtok = [k.sb(f"xtok{i}", [128, D], BF16) for i in range(2)]
    M_sb = [k.sb(f"Msb{i}", [128, 256], F32) for i in range(8)]
    M_bf = [k.sb(f"Mbf{i}", [128, 256], BF16) for i in range(8)]
    W2sb = [k.sb(f"W2sb{i}", [128, 256], BF16) for i in range(32)]
    qw_sb = [k.sb(f"qwsb{i}", [128, D], BF16) for i in range(8)]
    comb_t = [k.sb(f"combt{i}", [128, 256], BF16) for i in range(2)]
    Q_bf = [k.sb(f"Qbf{i}", [128, T2], BF16) for i in range(8)]
    zrow3 = [k.sb(f"zrow3_{d}", [128, TT], F32) for d in range(NC)]
    xres3 = [k.sb(f"xres3_{i}", [128, T2], F32) for i in range(2)]
    h_full = [k.sb(f"hfull{c}", [128, TT], F32) for c in range(NC)]
    x3tb = [k.sb(f"x3tb{i}", [128, TT], BF16) for i in range(2)]
    gplane = k.sb("gplane_t", [128, NC], F32)
    bplane = k.sb("bplane_t", [128, NC], F32)
    centT = k.sb("centT_t", [128, NC * 8], F32)
    sc_t = k.sb("sc_t", [128, 8], F32)

    pb = [k.ps(f"pb{i}", [128, TT], F32) for i in range(8)]
    ln = PostLN(k, "ln3", dict(sum=pb[4][0:1, :], sq=pb[5][0:1, :],
                               b1=pb[6][:], b2=pb[6][:]))
    scps = pb[7][:, 0:8]

    e_qw = []
    for a in range(4):
        for ec in range(2):
            e_qw.append(k.dma("gpsimd", qw_sb[a * 2 + ec][:],
                              io["queries"][(a * 256 + ec * 128) * D:
                                            (a * 256 + ec * 128 + 128) * D]
                              .rearrange("(p j) -> p j", p=128)))
    e_g = k.dma("sync", gplane[:], io["gplane"])
    e_b = k.dma("sync", bplane[:], io["bplane"])
    e_cent = k.dma("sync", centT[:], io["centT"])
    e_x2 = []
    for c in range(NC):
        e_x2.append(k.dma("sync", x2bf[c][:],
                          io["x2b"][c * 128:(c + 1) * 128, :]))

    e_mz = [k.op("vector", lambda e, i=i: e.memset(M_sb[i][:], 0.0))
            for i in range(8)]
    e_tp_use = [None, None]
    e_madd = {i: e_mz[i] for i in range(8)}
    e_bankM = [None, None]
    for u in range(T // 128):
        par = u % 2
        xt_ = xtok[par]
        e_tp = []
        for c in range(NC):
            e_tp.append(k.dma("scalar", xt_[:, c * 128:(c + 1) * 128],
                              x2bf[c][:, u * 128:(u + 1) * 128], transpose=True,
                              wait=[e_x2[c], e_tp_use[par]]))
        for q in range(4):
            for f0 in range(2):
                i = q * 2 + f0
                bank = pb[i % 2][:, 0:256]
                e_mm = k.op("tensor", lambda e, q=q, f0=f0, xt_=xt_, bank=bank:
                            e.matmul(bank, xt_[:, q * 256 + f0 * 128:
                                               q * 256 + f0 * 128 + 128],
                                     xt_[:, q * 256:q * 256 + 256],
                                     start=True, stop=True),
                            wait=[e_tp, e_bankM[i % 2]])
                e_ad = k.op("vector", lambda e, i=i, bank=bank: e.tensor_add(
                    M_sb[i][:], M_sb[i][:], bank),
                    wait=[e_mm, e_madd[i]])
                e_madd[i] = e_ad
                e_bankM[i % 2] = e_ad
        e_tp_use[par] = e_mm
    e_mcast = []
    for i in range(8):
        e_mcast.append(k.op("vector", lambda e, i=i: e.tensor_copy(
            M_bf[i][:], M_sb[i][:]), wait=[e_madd[i]]))
    e_bankQ_init = list(e_bankM)

    e_w2 = {}
    e_comb_use = [None, None]
    e_bankW = [None, None]
    for a in range(4):
        for q in range(4):
            for fc in range(2):
                idx = (a * 4 + q) * 2 + fc
                bank = pb[2 + idx % 2][:, 0:256]
                e_mm = None
                for g0 in range(2):
                    cpar = g0
                    src = io["combiners"][(a * 1024 + q * 256 + g0 * 128) * 256:
                                          (a * 1024 + q * 256 + g0 * 128 + 128) * 256]
                    src = src.rearrange("(p j) -> p j", p=128)
                    e_c = k.dma("gpsimd", comb_t[cpar][:], src,
                                wait=[e_comb_use[cpar]])
                    e_mm = k.op("tensor", lambda e, q=q, fc=fc, g0=g0, bank=bank,
                                ct=comb_t[cpar]: e.matmul(
                                    bank, M_bf[q * 2 + g0][:, fc * 128:fc * 128 + 128],
                                    ct[:], start=(g0 == 0), stop=(g0 == 1)),
                                wait=[e_c, e_mcast[q * 2 + g0]] +
                                     ([e_bankW[idx % 2]] if g0 == 0 else []))
                    e_comb_use[cpar] = e_mm
                widx = (a * 4 + q) * 2 + fc
                e_ev = k.op("vector", lambda e, widx=widx, bank=bank: e.tensor_copy(
                    W2sb[widx][:], bank), wait=[e_mm])
                e_w2[(a, q, fc)] = e_ev
                e_bankW[idx % 2] = e_ev

    e_zadds = {}
    e_qbf_use = {}
    e_bankQ = e_bankQ_init
    e_bankO = [e_w2[(3, 3, 0)], e_w2[(3, 3, 1)]]
    e_x3_dma = {}
    e_h_read = None
    e_sc_prev = None
    e_o3_prev = None
    e_xres_prev = [None, None]
    e_stats_prev = None

    def prev2(d_, t_):
        return (t_, d_ - 2) if d_ >= 2 else (t_ - 1, d_ + NC - 2)

    for t2 in range(NT2):
        tt = t2 // 2
        sub = t2 % 2
        for a in range(4):
            for q in range(4):
                for fc in range(2):
                    bank = pb[q % 2][:, 0:T2]
                    e_mm = None
                    for ec in range(2):
                        e_mm = k.op("tensor", lambda e, a=a, q=q, fc=fc, ec=ec,
                                    bank=bank, t2=t2: e.matmul(
                                        bank,
                                        qw_sb[a * 2 + ec][:, q * 256 + fc * 128:
                                                          q * 256 + fc * 128 + 128],
                                        x2bf[2 * a + ec][:, t2 * T2:(t2 + 1) * T2],
                                        start=(ec == 0), stop=(ec == 1)),
                                    wait=[e_qw[a * 2 + ec], e_x2[2 * a + ec]] +
                                         ([e_bankQ[q % 2]] if ec == 0 else []))
                    e_ev = k.op("vector", lambda e, q=q, fc=fc, bank=bank:
                                e.tensor_copy(Q_bf[q * 2 + fc][:], bank),
                                wait=[e_mm, e_qbf_use.get(q * 2 + fc)])
                    e_bankQ[q % 2] = e_ev
                    e_qbf_use[q * 2 + fc] = e_ev
            for g1 in range(2):
                bank = pb[2 + g1][:, 0:T2]
                e_mm = None
                for q in range(4):
                    for fc in range(2):
                        e_mm = k.op("tensor", lambda e, a=a, q=q, fc=fc, g1=g1,
                                    bank=bank: e.matmul(
                                        bank,
                                        W2sb[(a * 4 + q) * 2 + fc][:,
                                                                   g1 * 128:
                                                                   g1 * 128 + 128],
                                        Q_bf[q * 2 + fc][:],
                                        start=(q == 0 and fc == 0),
                                        stop=(q == 3 and fc == 1)),
                                    wait=[e_w2[(a, q, fc)],
                                          e_qbf_use[q * 2 + fc]] +
                                         ([e_bankO[g1]] if q == 0 and fc == 0
                                          else []))
                dch = 2 * a + g1
                xr = xres3[dch % 2]
                e_x = k.dma("scalar", xr[:],
                            io["x2f"][dch * 128:(dch + 1) * 128,
                                      t2 * T2:(t2 + 1) * T2],
                            wait=[e_xres_prev[dch % 2]])
                e_z = k.op("vector", lambda e, dch=dch, sub=sub, bank=bank, xr=xr:
                           e.tensor_add(zrow3[dch][:, sub * T2:(sub + 1) * T2],
                                        bank, xr[:]),
                           wait=[e_mm, e_x, e_stats_prev])
                e_xres_prev[dch % 2] = e_z
                e_bankO[g1] = e_z
                e_zadds[(t2, dch)] = e_z
        if sub == 0:
            continue

        h_evs = []

        def sink(dout, xt, ev, tt=tt, h_evs=h_evs):
            o1 = k.dma("sync", io["x3f"][dout * 128:(dout + 1) * 128,
                                         tt * TT:(tt + 1) * TT],
                       xt[:], wait=[ev])
            e_h = k.op("vector", lambda e, dout=dout, xt=xt: e.tensor_scalar(
                h_full[dout][:], xt[:], gplane[:, dout:dout + 1],
                bplane[:, dout:dout + 1], ALU.mult, ALU.add),
                wait=[ev, e_g, e_b, e_h_read])
            h_evs.append(e_h)
            xtb = x3tb[dout % 2]
            e_hb = k.op("vector", lambda e, dout=dout, xtb=xtb: e.tensor_copy(
                xtb[:], h_full[dout][:]),
                wait=[e_h, e_x3_dma.get(prev2(dout, tt) + (1,))])
            o2 = k.dma("sync", io["hb"][dout * 128:(dout + 1) * 128,
                                        tt * TT:(tt + 1) * TT],
                       xtb[:], wait=[e_hb])
            e_x3_dma[(tt, dout, 1)] = o2
            return [o1, e_h]

        e_sr, norm_evs = ln.tile(
            [zrow3[d_][:, 0:TT] for d_ in range(NC)],
            [[e_zadds[(tt * 2, d_)], e_zadds[(tt * 2 + 1, d_)]]
             for d_ in range(NC)], sink)
        e_stats_prev = e_sr

        for ts in range(4):
            e_mm = None
            for c in range(NC):
                e_mm = k.op("tensor", lambda e, c=c, ts=ts: e.matmul(
                    scps, h_full[c][:, ts * 128:(ts + 1) * 128],
                    centT[:, c * 8:(c + 1) * 8],
                    start=(c == 0), stop=(c == NC - 1)),
                    wait=[e_cent, h_evs] +
                         ([e_sc_prev] if c == 0 else []))
            e_sc = k.op("vector", lambda e: e.tensor_copy(sc_t[:], scps),
                        wait=[e_mm, e_o3_prev], strict=True, fence=True)
            e_sc_prev = e_sc
            o3 = k.dma("sync",
                       io["scores"][(tt * TT + ts * 128) * 8:
                                    (tt * TT + (ts + 1) * 128) * 8]
                       .rearrange("(p j) -> p j", p=128),
                       sc_t[:], wait=[e_sc])
            e_o3_prev = o3
            e_mm_last = e_mm
        e_h_read = e_mm_last

    k.op("vector", lambda e: e.engine_nop(),
         wait=[k.queue_done("sync"), k.queue_done("gpsimd"),
               k.queue_done("scalar")])


def build_A3(dbg=False):
    nc = bass.Bass("TRN2", target_bir_lowering=False)
    io = {}
    io["x2f"] = nc.dram_tensor("x2f", [D, T], F32, kind="ExternalInput").ap()
    io["x2b"] = nc.dram_tensor("x2b", [D, T], BF16, kind="ExternalInput").ap()
    io["queries"] = nc.dram_tensor("queries", [4 * 256 * D], F32,
                                   kind="ExternalInput").ap()
    io["combiners"] = nc.dram_tensor("combiners", [4 * D * 256], F32,
                                     kind="ExternalInput").ap()
    io["gplane"] = nc.dram_tensor("gplane", [128, NC], F32, kind="ExternalInput").ap()
    io["bplane"] = nc.dram_tensor("bplane", [128, NC], F32, kind="ExternalInput").ap()
    io["centT"] = nc.dram_tensor("centT", [128, NC * 8], F32,
                                 kind="ExternalInput").ap()
    io["x3f"] = nc.dram_tensor("x3f", [D, T], F32, kind="ExternalOutput").ap()
    io["hb"] = nc.dram_tensor("hb", [D, T], BF16, kind="ExternalOutput").ap()
    io["scores"] = nc.dram_tensor("scores", [T * 8], F32, kind="ExternalOutput").ap()
    k = KB(nc)
    emit_A3(k, io)
    k.build()
    return nc


def prep_A3_inputs(inputs, x2f, x2b):
    centT = np.ascontiguousarray(inputs["L1_ffn_centroids"].T)
    centp = np.ascontiguousarray(
        centT.reshape(NC, 128, 8).transpose(1, 0, 2).reshape(128, NC * 8))
    return {
        "x2f": x2f, "x2b": x2b,
        "queries": inputs["L1_attn_queries"].reshape(-1),
        "combiners": inputs["L1_attn_combiners"].reshape(-1),
        "gplane": _plane(inputs["L1_ffn_ln_g"]),
        "bplane": _plane(inputs["L1_ffn_ln_b"]),
        "centT": centp,
    }



CAP = 768
NSL = CAP // 128
CGLOB = 4096.0
BIGF = 8388608.0
NEG = -1.0e38
TAU_IT = 32
TH_IT = 17


def emit_B(k, io):
    nc = k.nc
    _regs = {}

    def breg(e, val):
        if val not in _regs:
            _regs[val] = e.to_reg(val)
        return _regs[val]

    scw = [k.sb(f"scw{e}", [128, 256], F32) for e in range(8)]
    taken = k.sb("taken", [128, 256], F32)
    iotag = k.sb("iotag_t", [128, 256], F32)
    s_av = k.sb("s_av", [128, 256], F32)
    s_mn = k.sb("s_mn", [128, 256], F32)
    selg = k.sb("selg", [128, 256], F32)
    eqm = k.sb("eqm", [128, 256], F32)
    tjunk = k.sb("tjunk", [128, 256], F32)
    ones256 = k.sb("ones256", [128, 256], F32)
    cntc = k.sb("cntc", [128, 1], F32)
    red1 = k.sb("red1", [128, 1], F32)
    rrow_d = k.sb("rrow_d", [1, 128], F32)
    sc_lo = [k.sb(f"sc_lo{i}", [1, 1], F32) for i in range(2)]
    sc_hi = [k.sb(f"sc_hi{i}", [1, 1], F32) for i in range(2)]
    sc_mid = k.sb("sc_mid", [1, 1], F32)
    sc_t1 = k.sb("sc_t1", [1, 1], F32)
    sc_t2 = k.sb("sc_t2", [1, 1], F32)
    sc_cmp = k.sb("sc_cmp", [1, 1], F32)
    sc_m = k.sb("sc_m", [1, 1], F32)
    ones_row_b = k.sb("ones_row_b", [1, 128], F32)
    ones_col_b = k.sb("ones_col_b", [128, 1], F32)
    ltri = k.sb("ltri_t", [128, 128], F32)
    eyeb = k.sb("eyeb_t", [128, 128], BF16)
    tlid_f = k.sb("tlidf_t", [128, 32], F32)
    tidf = [k.sb(f"tidf{i}", [128, 128], F32) for i in range(2)]
    fidxf = [k.sb(f"fidxf{i}", [128, 1], F32) for i in range(2)]
    incl = k.sb("incl", [128, 32], F32)
    excl = k.sb("excl", [128, 32], F32)
    dbgrow = k.sb("dbgrow", [1, 32], F32) if io.get("dbg") is not None else None
    posr = k.sb("posr", [128, 32], F32)
    posall = k.sb("posall", [128, 32], F32)
    pos_i = k.sb("pos_i", [128, 32], I32)
    basec = k.sb("basec", [128, 1], F32)

    w1sb = [k.sb(f"w1sb{c}", [128, 4096], BF16) for c in range(NC)]
    w2t = [k.sb(f"w2t{i}", [128, D], BF16) for i in range(4)]
    relu1 = [k.sb(f"rl{i}", [128, 256], BF16) for i in range(32)]
    h_rows = [k.sb(f"hrow{i}", [128, D], BF16) for i in range(2)]
    x3_rows = [k.sb(f"x3row{i}", [128, D], F32) for i in range(2)]
    h_fm = [k.sb(f"hfm{c}", [128, 256], BF16) for c in range(NC)]
    idxc = [k.sb(f"idxc{i}", [128, 1], I32) for i in range(2)]
    b1p = k.sb("b1p_t", [128, 256], F32)
    b2sb = k.sb("b2sb", [128, D], F32)
    b2row = k.sb("b2row_t", [1, D], F32)
    z_sb = [k.sb(f"zsb{i}", [128, D], F32) for i in range(2)]
    zt = [k.sb(f"zt{i}", [128, D], F32) for i in range(2)]
    ot = [k.sb(f"ot{i}", [128, D], F32) for i in range(2)]
    fsc = k.sb("fsc", [128, D], F32)
    fsum = k.sb("fsum", [128, 1], F32)
    fsq = k.sb("fsq", [128, 1], F32)
    fm = k.sb("fm", [128, 1], F32)
    fv = k.sb("fv", [128, 1], F32)
    fr = k.sb("fr", [128, 1], F32)
    eps_colB = k.sb("eps_colB", [128, 1], F32)

    py1 = [k.ps(f"py1_{i}", [128, TT], F32) for i in range(2)]
    py2 = [k.ps(f"py2_{i}", [128, TT], F32) for i in range(4)]
    pbis = k.ps("pbis", [128, TT], F32)
    ptp = k.ps("ptp", [128, 128], BF16)
    midb = pbis[:, 0:1]
    cnt1 = pbis[0:1, 4:5]
    baseps = pbis[:, 8:9]

    e_sc = [k.dma("sync", scw[e][:], io["scoresR"][e * 128 * 256:(e + 1) * 128 * 256]
                  .rearrange("(p f) -> p f", p=128)) for e in range(8)]
    e_io = k.dma("sync", iotag[:], io["iotag"])
    e_lt = k.dma("sync", ltri[:], io["ltri"])
    e_ey = k.dma("sync", eyeb[:], io["eyeb"])
    e_tlf = k.dma("sync", tlid_f[:], io["tlid_f"])
    e_b1 = k.dma("sync", b1p[:], io["b1p"])
    e_orow = k.op("vector", lambda e: e.memset(ones_row_b[:], 1.0))
    e_epsB = k.op("vector", lambda e: e.memset(eps_colB[:], LN_EPS))
    e_o256 = k.op("vector", lambda e: e.memset(ones256[:], 1.0))
    e_ocol = k.op("vector", lambda e: e.memset(ones_col_b[:], 1.0))
    e_tk = k.op("vector", lambda e: e.memset(taken[:], 0.0))
    e_pa = k.op("vector", lambda e: e.memset(posall[:], BIGF))



    def vop(fn, wait=()):
        return k.op("vector", fn, wait=wait, strict=True, fence=True)

    def colred(src_ap, op, wait):
        e1 = vop(lambda e: e.tensor_reduce(red1[:], src_ap, AX.X, op), wait)
        e2 = k.dma("gpsimd", io["bounce"].rearrange("(p a) -> p a", a=1),
                   red1[:], wait=[e1])
        e3 = k.dma("gpsimd", rrow_d[:], io["bounce"].rearrange("(a p) -> a p", a=1),
                   wait=[e2])
        e4 = vop(lambda e: e.tensor_reduce(sc_t1[0:1, 0:1], rrow_d[:], AX.X, op),
                 [e3])
        return e4

    e_selprev = None
    e_posall = e_pa
    e_scatter_done = []
    e_sgd = None
    for rnd in range(8):
        if rnd < 7:
            ea = vop(lambda e, rnd=rnd: e.tensor_mul(tjunk[:], scw[rnd][:], taken[:]),
                     [e_sc[rnd], e_tk, e_selprev])
            eb = vop(lambda e, rnd=rnd: e.tensor_sub(s_av[:], scw[rnd][:], tjunk[:]))
            ec = vop(lambda e: e.scalar_tensor_tensor(
                s_av[:], taken[:], NEG, s_av[:], ALU.mult, ALU.add))
            ed = vop(lambda e, rnd=rnd: e.scalar_tensor_tensor(
                s_mn[:], taken[:], 1.0e38, scw[rnd][:], ALU.mult, ALU.add))
            e_hi0 = colred(s_av[:], ALU.max, [ec])
            eh = vop(lambda e: e.tensor_copy(sc_hi[0][:], sc_t1[0:1, 0:1]), [e_hi0])
            e_lo0 = colred(s_mn[:], ALU.min, [ed, eh])
            el = vop(lambda e: e.tensor_scalar_add(sc_lo[0][:], sc_t1[0:1, 0:1],
                                                   -1.0), [e_lo0])
            e_it = [el, eh]
            e_cntev = None
            for it in range(TAU_IT):
                pl, ph = sc_lo[it % 2], sc_hi[it % 2]
                nl, nh = sc_lo[1 - it % 2], sc_hi[1 - it % 2]
                em1 = vop(lambda e, pl=pl, ph=ph: e.tensor_add(sc_mid[:], pl[:],
                                                               ph[:]), e_it)
                em2 = vop(lambda e: e.tensor_scalar_mul(sc_mid[:], sc_mid[:], 0.5))
                eb1 = k.op("tensor", lambda e: e.matmul(
                    midb, ones_row_b[:], sc_mid[:], start=True, stop=True),
                    wait=[em2, e_orow, e_cntev])
                ecn = vop(lambda e: e.scalar_tensor_tensor(
                    tjunk[:], s_av[:], midb, ones256[:], ALU.is_gt, ALU.mult,
                    accum_out=cntc[:]), [eb1, ec, e_o256])
                ecm = k.op("tensor", lambda e: e.matmul(
                    cnt1, cntc[:], ones_col_b[:], start=True, stop=True),
                    wait=[ecn, e_ocol])
                e_cntev = ecm
                ecp = vop(lambda e: e.tensor_single_scalar(
                    sc_cmp[:], cnt1, CGLOB, ALU.is_ge), [ecm])
                eu1 = vop(lambda e, pl=pl: e.tensor_sub(sc_t1[0:1, 0:1], sc_mid[:],
                                                        pl[:]), [ecp])
                eu2 = vop(lambda e: e.tensor_mul(sc_t1[0:1, 0:1], sc_t1[0:1, 0:1],
                                                 sc_cmp[:]))
                eu3 = vop(lambda e, pl=pl, nl=nl: e.tensor_add(nl[:], pl[:],
                                                               sc_t1[0:1, 0:1]))
                eu4 = vop(lambda e, ph=ph: e.tensor_sub(sc_t2[:], ph[:], sc_mid[:]))
                eu5 = vop(lambda e: e.tensor_mul(sc_t2[:], sc_t2[:], sc_cmp[:]))
                eu6 = vop(lambda e, nh=nh: e.tensor_add(nh[:], sc_mid[:], sc_t2[:]))
                e_it = [eu3, eu6]
            tau = sc_hi[TAU_IT % 2]
            eb2 = k.op("tensor", lambda e, tau=tau: e.matmul(
                midb, ones_row_b[:], tau[:], start=True, stop=True),
                wait=e_it + [e_cntev])
            eg1 = vop(lambda e: e.scalar_tensor_tensor(
                selg[:], s_av[:], midb, ones256[:], ALU.is_gt, ALU.mult,
                accum_out=cntc[:]), [eb2, e_o256, e_sgd])
            eg2 = k.op("tensor", lambda e: e.matmul(
                cnt1, cntc[:], ones_col_b[:], start=True, stop=True), wait=[eg1])
            egm = vop(lambda e: e.tensor_scalar(
                sc_m[:], cnt1, -1.0, CGLOB, ALU.mult, ALU.add), [eg2])
            ee1 = vop(lambda e: e.tensor_scalar(
                eqm[:], s_av[:], midb, None, ALU.is_equal), [eb2, eg1])
            if io.get("dbg") is not None:
                vop(lambda e, rnd=rnd, tau=tau: e.tensor_copy(
                    dbgrow[0:1, rnd * 4:rnd * 4 + 1], tau[:]), [eb2])
                vop(lambda e, rnd=rnd: e.tensor_copy(
                    dbgrow[0:1, rnd * 4 + 1:rnd * 4 + 2], sc_m[:]), [egm])
            ei1 = vop(lambda e: e.memset(sc_lo[0][:], -1.0), [eb2, eg2])
            ei2 = vop(lambda e: e.memset(sc_hi[0][:], 40000.0), [eb2, eg2])
            e_it = [ei1, ei2, ee1, egm, e_io]
            e_cntev = eg2
            for it in range(TH_IT):
                pl, ph = sc_lo[it % 2], sc_hi[it % 2]
                nl, nh = sc_lo[1 - it % 2], sc_hi[1 - it % 2]
                em1 = vop(lambda e, pl=pl, ph=ph: e.tensor_add(sc_mid[:], pl[:],
                                                               ph[:]), e_it)
                em2 = vop(lambda e: e.tensor_scalar_mul(sc_mid[:], sc_mid[:], 0.5))
                eb1 = k.op("tensor", lambda e: e.matmul(
                    midb, ones_row_b[:], sc_mid[:], start=True, stop=True),
                    wait=[em2, e_cntev])
                ecn = vop(lambda e: e.scalar_tensor_tensor(
                    tjunk[:], iotag[:], midb, eqm[:], ALU.is_le, ALU.mult,
                    accum_out=cntc[:]), [eb1])
                ecm = k.op("tensor", lambda e: e.matmul(
                    cnt1, cntc[:], ones_col_b[:], start=True, stop=True),
                    wait=[ecn])
                e_cntev = ecm
                ecp = vop(lambda e: e.tensor_single_scalar(
                    sc_cmp[:], cnt1, sc_m[0:1, 0:1], ALU.is_lt), [ecm])
                eu1 = vop(lambda e, pl=pl: e.tensor_sub(sc_t1[0:1, 0:1], sc_mid[:],
                                                        pl[:]), [ecp])
                eu2 = vop(lambda e: e.tensor_mul(sc_t1[0:1, 0:1], sc_t1[0:1, 0:1],
                                                 sc_cmp[:]))
                eu3 = vop(lambda e, pl=pl, nl=nl: e.tensor_add(nl[:], pl[:],
                                                               sc_t1[0:1, 0:1]))
                eu4 = vop(lambda e, ph=ph: e.tensor_sub(sc_t2[:], ph[:], sc_mid[:]))
                eu5 = vop(lambda e: e.tensor_mul(sc_t2[:], sc_t2[:], sc_cmp[:]))
                eu6 = vop(lambda e, nh=nh: e.tensor_add(nh[:], sc_mid[:], sc_t2[:]))
                e_it = [eu3, eu6]
            th = sc_hi[TH_IT % 2]
            eb3 = k.op("tensor", lambda e, th=th: e.matmul(
                midb, ones_row_b[:], th[:], start=True, stop=True),
                wait=e_it + [e_cntev])
            es1 = vop(lambda e: e.scalar_tensor_tensor(
                tjunk[:], iotag[:], midb, eqm[:], ALU.is_le, ALU.mult), [eb3])
            es2 = vop(lambda e: e.tensor_add(selg[:], selg[:], tjunk[:]))
            e_sel = es2
            if io.get("dbg") is not None:
                vop(lambda e, rnd=rnd, th=th: e.tensor_copy(
                    dbgrow[0:1, rnd * 4 + 2:rnd * 4 + 3], th[:]), [eb3])
        else:
            e_sel = vop(lambda e: e.tensor_scalar(
                selg[:], taken[:], -1.0, 1.0, ALU.mult, ALU.add),
                [e_selprev, e_sgd])
        if io.get("dbg") is not None:
            e_sgd = k.dma("gpsimd", io["dbg"]["selg"]
                          [rnd * 32768:(rnd + 1) * 32768]
                          .rearrange("(p f) -> p f", p=128), selg[:],
                          wait=[e_sel])
        e_tk = vop(lambda e: e.tensor_add(taken[:], taken[:], selg[:]), [e_sel])
        e_selprev = e_tk
        sel_loc = selg[:, 0:32]
        ec1 = vop(lambda e: e.tensor_tensor_scan(
            incl[:], ones256[:, 0:32], sel_loc, 0.0,
            ALU.mult, ALU.add), [e_sel, e_o256])
        ec2 = vop(lambda e: e.tensor_sub(excl[:], incl[:], sel_loc))
        ecb = k.op("tensor", lambda e: e.matmul(
            baseps, ltri[:], incl[:, 31:32], start=True, stop=True),
            wait=[ec1, e_lt])
        e_basec = vop(lambda e: e.tensor_copy(basec[:], baseps), [ecb])
        ec3 = vop(lambda e, rnd=rnd: e.tensor_scalar(
            posr[:], excl[:], basec[:, 0:1], float(rnd * CAP), ALU.add, ALU.add),
            [e_basec, ec2])
        ec4 = vop(lambda e: e.scalar_tensor_tensor(
            posr[:], sel_loc, -BIGF, posr[:], ALU.mult, ALU.add), [ec3])
        ec5 = vop(lambda e: e.tensor_scalar_add(posr[:], posr[:], BIGF))
        ec6 = vop(lambda e: e.tensor_tensor(posall[:], posall[:], posr[:], ALU.min),
                  [ec5, e_posall])
        e_posall = ec6
    dbg = io.get("dbg")
    if dbg is not None:
        k.dma("gpsimd", dbg["posall"], posall[:], wait=[e_posall])
        k.dma("gpsimd", dbg["incl"], incl[:], wait=[e_posall])
        k.dma("gpsimd", dbg["basec"], basec[:], wait=[e_posall])
        k.dma("gpsimd", dbg["tauscal"], dbgrow[:],
              wait=k.queue_done("vector"))
    ep1 = vop(lambda e: e.tensor_copy(pos_i[:], posall[:]), [e_posall])
    e_bigz = vop(lambda e: e.memset(fsc[:], BIGF), [])
    e_z0 = vop(lambda e: e.memset(z_sb[0][:], 0.0), [])
    e_z0b = vop(lambda e: e.memset(h_rows[0][:], 0.0), [])
    e_init = []
    for i in range(6):
        e_init.append(k.dma("sync", io["bucket_tid"][i * 131072:(i + 1) * 131072]
                            .rearrange("(p j) -> p j", p=128), fsc[:],
                            wait=[e_bigz]))
    for i in range(48):
        e_init.append(k.dma("sync", io["bucket_x3"][i * 131072:(i + 1) * 131072]
                            .rearrange("(p j) -> p j", p=128), z_sb[0][:],
                            wait=[e_z0]))
    for i in range(48):
        e_init.append(k.dma("sync", io["bucket_h"][i * 131072:(i + 1) * 131072]
                            .rearrange("(p j) -> p j", p=128), h_rows[0][:],
                            wait=[e_z0b]))
    e_scat_prev = {}
    for f in range(32):
        par = f % 2
        e_lh = k.dma("sync", h_rows[par][:], io["htm"][f * 128:(f + 1) * 128, :],
                     wait=[e_scat_prev.get(("h", par)), e_init])
        e_lx = k.dma("sync", x3_rows[par][:], io["x3tm"][f * 128:(f + 1) * 128, :],
                     wait=[e_scat_prev.get(("x", par)), e_init])
        e_tf = vop(lambda e, par=par, f=f: e.tensor_scalar_mul(
            tidf[par][:], ones256[:, 0:128], tlid_f[:, f:f + 1]),
            [e_tlf, e_o256, e_scat_prev.get(("t", par))])
        e_s1 = k.op("gpsimd", lambda e, par=par, f=f: e.indirect_dma_start(
            io["bucket_h"].rearrange("(r j) -> r j", j=D),
            bass.IndirectOffsetOnAxis(pos_i[:, f:f + 1], 0),
            h_rows[par][:], None,
            bounds_check=breg(e, 8 * CAP - 1), oob_is_err=False),
            wait=[ep1, e_lh, e_init], amt=16)
        e_s2 = k.op("gpsimd", lambda e, par=par, f=f: e.indirect_dma_start(
            io["bucket_x3"].rearrange("(r j) -> r j", j=D),
            bass.IndirectOffsetOnAxis(pos_i[:, f:f + 1], 0),
            x3_rows[par][:], None,
            bounds_check=breg(e, 8 * CAP - 1), oob_is_err=False),
            wait=[ep1, e_lx, e_init], amt=16)
        e_s3 = k.op("gpsimd", lambda e, par=par, f=f: e.indirect_dma_start(
            io["bucket_tid"].rearrange("(r j) -> r j", j=128),
            bass.IndirectOffsetOnAxis(pos_i[:, f:f + 1], 0),
            tidf[par][:], None,
            bounds_check=breg(e, 8 * CAP - 1), oob_is_err=False),
            wait=[ep1, e_tf, e_init], amt=16)
        e_scat_prev[("h", par)] = e_s1
        e_scat_prev[("x", par)] = e_s2
        e_scat_prev[("t", par)] = e_s3
        e_scatter_done = [e_s1, e_s2, e_s3]

    e_w1use = None
    e_w2use = [None] * 4
    e_gat_use = {}
    e_idx_use = {}
    e_hfm_use = None
    e_b2b = None
    e_zdma = {0: e_init, 1: e_init}
    e_y1bank = [None, None]
    e_y2bank = [None] * 4
    e_tp_done = None
    for ei in range(8):
        e_w1 = []
        for c in range(NC):
            e_w1.append(k.dma("gpsimd", w1sb[c][:],
                              io["w1"][(ei * D + c * 128) * 4096:
                                       (ei * D + c * 128 + 128) * 4096]
                              .rearrange("(p j) -> p j", p=128),
                              wait=[e_w1use]))
        e_b2r = k.dma("sync", b2row[:], io["b2"][ei * D:(ei + 1) * D]
                      .rearrange("(a j) -> a j", a=1), wait=[e_b2b])
        e_bb = []
        for hh in range(2):
            e_bb.append(k.op("tensor", lambda e, hh=hh: e.matmul(
                py1[hh][:], ones_row_b[:], b2row[:, hh * TT:(hh + 1) * TT],
                start=True, stop=True),
                wait=[e_b2r, e_orow, e_y1bank[hh]]))
        e_b2c = []
        for hh in range(2):
            e_b2c.append(k.op("vector", lambda e, hh=hh: e.tensor_copy(
                b2sb[:, hh * TT:(hh + 1) * TT], py1[hh][:]), wait=[e_bb[hh]]))
        e_y1bank = list(e_b2c)
        e_b2b = e_b2c

        e_y2_last = None
        e_gx_by = {}
        for pair in range(3):
            sts = [pair * 2, pair * 2 + 1]
            e_relu = {}
            e_tpc = {c: [] for c in range(NC)}
            for j, st in enumerate(sts):
                par = st % 2
                e_gh = k.dma("sync", h_rows[par][:],
                             io["bucket_h"][(ei * CAP + st * 128) * D:
                                            (ei * CAP + (st + 1) * 128) * D]
                             .rearrange("(p j) -> p j", p=128),
                             wait=[e_scatter_done, e_gat_use.get(("h", par))])
                e_gx = k.dma("sync", x3_rows[par][:],
                             io["bucket_x3"][(ei * CAP + st * 128) * D:
                                             (ei * CAP + (st + 1) * 128) * D]
                             .rearrange("(p j) -> p j", p=128),
                             wait=[e_scatter_done, e_gat_use.get(("x", par))])
                e_gx_by[par] = e_gx
                for c in range(NC):
                    e_t1 = k.op("tensor", lambda e, c=c, par=par: e.transpose(
                        ptp[:], h_rows[par][:, c * 128:(c + 1) * 128], eyeb[:]),
                        wait=[e_gh, e_ey, e_tp_done])
                    e_t2 = k.op("vector", lambda e, c=c, j=j: e.tensor_copy(
                        h_fm[c][:, j * 128:(j + 1) * 128], ptp[:]),
                        wait=[e_t1, e_hfm_use])
                    e_tp_done = e_t2
                    e_tpc[c].append(e_t2)
                e_gat_use[("h", par)] = e_tpc[NC - 1][-1]
            for hd in range(32):
                bank = py1[hd % 2]
                e_mm = None
                for c in range(NC):
                    e_mm = k.op("tensor", lambda e, c=c, hd=hd, bank=bank:
                                e.matmul(bank[:, 0:256],
                                         w1sb[c][:, hd * 128:(hd + 1) * 128],
                                         h_fm[c][:],
                                         start=(c == 0), stop=(c == NC - 1)),
                                wait=[e_w1[c], e_tpc[c]] +
                                     ([e_y1bank[hd % 2]] if c == 0 else []))
                e_ev = k.op("scalar", lambda e, hd=hd, ei=ei, bank=bank:
                            e.activation(relu1[hd][:], bank[:, 0:256],
                                         AF.Gelu_apprx_tanh,
                                         bias=b1p[:, ei * 32 + hd:
                                                  ei * 32 + hd + 1]),
                            wait=[e_mm, e_b1, e_y2_last])
                e_relu[hd] = e_ev
                e_y1bank[hd % 2] = e_ev
            e_hfm_use = e_mm
            e_mm4 = [None] * 4
            for hd in range(32):
                wt = w2t[hd % 4]
                e_w2 = k.dma("gpsimd", wt[:],
                             io["w2"][(ei * 4096 + hd * 128) * D:
                                      (ei * 4096 + hd * 128 + 128) * D]
                             .rearrange("(p j) -> p j", p=128),
                             wait=[e_w2use[hd % 4]])
                for j in range(2):
                    for hh in range(2):
                        gi = j * 2 + hh
                        e_mm4[gi] = k.op("tensor", lambda e, j=j, hh=hh, hd=hd,
                                         wt=wt, gi=gi: e.matmul(
                                             py2[gi][:],
                                             relu1[hd][:, j * 128:(j + 1) * 128],
                                             wt[:, hh * TT:(hh + 1) * TT],
                                             start=(hd == 0), stop=(hd == 31)),
                                         wait=[e_w2, e_relu[hd]] +
                                              ([e_y2bank[gi]] if hd == 0 else []))
                e_w2use[hd % 4] = e_mm4[3]
            e_y2_last = e_mm4[3]
            for j, st in enumerate(sts):
                par = st % 2
                zb = z_sb[par]
                ezz = []
                for hh in range(2):
                    gi = j * 2 + hh
                    sl = slice(hh * TT, (hh + 1) * TT)
                    e_z1 = k.op("vector", lambda e, par=par, gi=gi, sl=sl, zb=zb:
                                e.scalar_tensor_tensor(
                                    zb[:, sl], x3_rows[par][:, sl], 2.0,
                                    py2[gi][:], ALU.mult, ALU.add),
                                wait=[e_mm4[gi], e_gx_by[par],
                                      e_zdma.get(par)])
                    e_z2 = k.op("vector", lambda e, sl=sl, zb=zb: e.tensor_add(
                        zb[:, sl], zb[:, sl], b2sb[:, sl]),
                        wait=[e_z1, e_b2c])
                    ezz.append(e_z2)
                    e_y2bank[gi] = e_z1
                e_zd = k.dma("sync", io["z_d"][(ei * CAP + st * 128) * D:
                                               (ei * CAP + (st + 1) * 128) * D]
                             .rearrange("(p j) -> p j", p=128), zb[:],
                             wait=ezz)
                e_zdma[par] = e_zd
                e_gat_use[("x", par)] = ezz[-1]
        e_w1use = e_mm
    e_f_prev = [None, None]
    e_oprev = [None, None]
    for ei in range(8):
        for st in range(NSL):
            par = st % 2
            e_zl = k.dma("scalar", zt[par][:],
                         io["z_d"][(ei * CAP + st * 128) * D:
                                   (ei * CAP + (st + 1) * 128) * D]
                         .rearrange("(p j) -> p j", p=128),
                         wait=[k.queue_done("sync"), e_f_prev[par]])
            e_ilf = k.dma("scalar", fidxf[par][:],
                          io["bucket_tid"].rearrange("(r j) -> r j", j=128)
                          [ei * CAP + st * 128: ei * CAP + (st + 1) * 128, 0:1],
                          wait=[e_idx_use.get(par), e_oprev[par],
                                e_scatter_done],
                          allow_slow_non_contiguous=True)
            e_il = vop(lambda e, par=par: e.tensor_copy(idxc[par][:],
                                                        fidxf[par][:]), [e_ilf])
            e_idx_use[par] = e_il
            e_s1 = vop(lambda e, par=par: e.reduce_sum(fsum[:], zt[par][:],
                                                       axis=AX.X), [e_zl])
            e_s2 = vop(lambda e, par=par: e.scalar_tensor_tensor(
                fsc[:], zt[par][:], 1.0, zt[par][:], ALU.mult, ALU.mult,
                accum_out=fsq[:]), [e_zl])
            e_m1 = vop(lambda e: e.tensor_scalar_mul(fm[:], fsum[:], 1.0 / D),
                       [e_s1])
            e_v1 = vop(lambda e: e.tensor_scalar_mul(fv[:], fsq[:], 1.0 / D),
                       [e_s2])
            e_v2 = vop(lambda e: e.tensor_mul(fr[:], fm[:], fm[:]), [e_m1])
            e_v3 = vop(lambda e: e.tensor_sub(fv[:], fv[:], fr[:]), [e_v2, e_v1])
            e_l1 = k.op("scalar", lambda e: e.activation(
                fr[:], fv[:], AF.Ln, bias=eps_colB[:, 0:1]),
                wait=[e_v3, e_epsB])
            e_l2 = k.op("scalar", lambda e: e.activation(
                fr[:], fr[:], AF.Exp, scale=-0.5))
            e_n1 = vop(lambda e, par=par: e.tensor_scalar(
                ot[par][:], zt[par][:], fm[:, 0:1], fr[:, 0:1],
                ALU.subtract, ALU.mult), [e_l2, e_oprev[par]])
            e_sc2 = k.op("gpsimd", lambda e, par=par: e.indirect_dma_start(
                io["out_tm"],
                bass.IndirectOffsetOnAxis(idxc[par][:, 0:1], 0),
                ot[par][:], None,
                bounds_check=breg(e, T - 1), oob_is_err=False),
                wait=[e_n1, e_il], amt=16)
            e_oprev[par] = e_sc2
            e_f_prev[par] = e_n1
    k.op("vector", lambda e: e.engine_nop(),
         wait=[k.queue_done("sync"), k.queue_done("gpsimd"),
               k.queue_done("scalar")])


def build_B(dbg=False):
    nc = bass.Bass("TRN2", target_bir_lowering=False)
    io = {}
    if dbg:
        io["dbg"] = {
            "posall": nc.dram_tensor("dbg_posall", [128, 32], F32,
                                     kind="ExternalOutput").ap(),
            "incl": nc.dram_tensor("dbg_incl", [128, 32], F32,
                                   kind="ExternalOutput").ap(),
            "basec": nc.dram_tensor("dbg_basec", [128, 1], F32,
                                    kind="ExternalOutput").ap(),
            "tauscal": nc.dram_tensor("dbg_tauscal", [1, 32], F32,
                                      kind="ExternalOutput").ap(),
            "selg": nc.dram_tensor("dbg_selg", [8 * 128 * 256], F32,
                                   kind="ExternalOutput").ap(),
        }
    io["scoresR"] = nc.dram_tensor("scoresR", [8 * 128 * 256], F32,
                                   kind="ExternalInput").ap()
    io["iotag"] = nc.dram_tensor("iotag", [128, 256], F32, kind="ExternalInput").ap()
    io["ltri"] = nc.dram_tensor("ltri", [128, 128], F32, kind="ExternalInput").ap()
    io["eyeb"] = nc.dram_tensor("eyeb", [128, 128], BF16, kind="ExternalInput").ap()
    io["tlid_f"] = nc.dram_tensor("tlid_f", [128, 32], F32, kind="ExternalInput").ap()
    io["b1p"] = nc.dram_tensor("b1p", [128, 256], F32, kind="ExternalInput").ap()
    io["b2"] = nc.dram_tensor("b2", [8 * D], F32, kind="ExternalInput").ap()
    io["w1"] = nc.dram_tensor("w1", [8 * D * 4096], F32, kind="ExternalInput").ap()
    io["w2"] = nc.dram_tensor("w2", [8 * 4096 * D], F32, kind="ExternalInput").ap()
    io["htm"] = nc.dram_tensor("htm", [T, D], BF16, kind="ExternalInput").ap()
    io["x3tm"] = nc.dram_tensor("x3tm", [T, D], F32, kind="ExternalInput").ap()
    io["out_tm"] = nc.dram_tensor("out_tm", [T, D], F32, kind="ExternalOutput").ap()
    io["bucket_h"] = nc.dram_tensor("bucket_h", [8 * CAP * D], BF16,
                                    kind="Internal").ap()
    io["bucket_x3"] = nc.dram_tensor("bucket_x3", [8 * CAP * D], F32,
                                     kind="Internal").ap()
    io["bucket_tid"] = nc.dram_tensor("bucket_tid", [8 * CAP * 128], F32,
                                      kind="ExternalOutput" if dbg else "Internal").ap()
    io["z_d"] = nc.dram_tensor("z_d", [8 * CAP * D], F32, kind="Internal").ap()
    io["bounce"] = nc.dram_tensor("bounce", [128], F32, kind="Internal").ap()
    k = KB(nc)
    emit_B(k, io)
    k.build()
    return nc


def prep_B_inputs(inputs, core, scores_all, x3tm, htm):
    own = np.arange(core * T, (core + 1) * T)
    rest = np.concatenate([np.arange(0, core * T),
                           np.arange((core + 1) * T, NCORES * T)])
    ordc = np.concatenate([own, rest])
    S = scores_all[ordc].reshape(256, 128, 8).transpose(2, 1, 0)
    iotag = np.ascontiguousarray(
        ordc.reshape(256, 128).T).astype(np.float32)
    ltri = np.tril(np.ones((128, 128), dtype=np.float32), -1).T
    ltri = np.ascontiguousarray(ltri)
    tlid = np.ascontiguousarray(
        (np.arange(32)[None, :] * 128 + np.arange(128)[:, None])).astype(np.int32)
    b1 = inputs["L1_ffn_b1"]
    b1p = np.ascontiguousarray(
        b1.reshape(8, 32, 128).transpose(2, 0, 1).reshape(128, 256)).astype(np.float32)
    return {
        "scoresR": np.ascontiguousarray(S.reshape(-1)).astype(np.float32),
        "iotag": iotag,
        "ltri": ltri,
        "eyeb": np.eye(128, dtype=np.float32).astype(BF),
        "tlid_f": tlid.astype(np.float32),
        "b1p": b1p,
        "b2": inputs["L1_ffn_b2"].reshape(-1),
        "w1": inputs["L1_ffn_w1"].reshape(-1),
        "w2": inputs["L1_ffn_w2"].reshape(-1),
        "htm": htm,
        "x3tm": x3tm,
    }



def _plane(v):
    return np.ascontiguousarray(v.reshape(NC, 128).T).astype(np.float32)


def prep_A1_inputs(inputs, core):
    x = inputs["L0_attn_x"][core]
    return {
        "x_cm": np.ascontiguousarray(x.T),
        "wkT": np.ascontiguousarray(inputs["L0_attn_Wk"].T),
        "wvT": np.ascontiguousarray(inputs["L0_attn_Wv"].T),
        "wrT": np.ascontiguousarray(inputs["L0_attn_Wr"].T),
        "woT": np.ascontiguousarray(inputs["L0_attn_Wo"].T),
        "tmk": _plane(inputs["L0_attn_time_mix_k"].reshape(-1)),
        "tmv": _plane(inputs["L0_attn_time_mix_v"].reshape(-1)),
        "tmr": _plane(inputs["L0_attn_time_mix_r"].reshape(-1)),
        "tdec": _plane(inputs["L0_attn_time_decay"]),
        "tfir": _plane(inputs["L0_attn_time_first"]),
    }


def kernel(**inputs):
    cores = list(range(NCORES))

    nc1 = build_A1()
    res1 = run_bass_kernel_spmd(nc1, [prep_A1_inputs(inputs, c) for c in cores],
                                core_ids=cores)
    nc2 = build_A2()
    maps2 = [prep_A2_inputs(inputs, np.asarray(res1.results[c]["x1f"]),
                            np.asarray(res1.results[c]["x1b"]))
             for c in cores]
    res2 = run_bass_kernel_spmd(nc2, maps2, core_ids=cores)
    nc3 = build_A3()
    maps3 = [prep_A3_inputs(inputs, np.asarray(res2.results[c]["x2f"]),
                            np.asarray(res2.results[c]["x2b"]))
             for c in cores]
    res3 = run_bass_kernel_spmd(nc3, maps3, core_ids=cores)
    scores_all = np.concatenate(
        [np.asarray(res3.results[c]["scores"]).reshape(T, 8) for c in cores])
    x3tm = [np.ascontiguousarray(np.asarray(res3.results[c]["x3f"]).T)
            for c in cores]
    htm = [np.ascontiguousarray(np.asarray(res3.results[c]["hb"]).T)
           for c in cores]
    ncb = build_B()
    mapsb = [prep_B_inputs(inputs, c, scores_all, x3tm[c], htm[c])
             for c in cores]
    resb = run_bass_kernel_spmd(ncb, mapsb, core_ids=cores)
    return np.stack([np.asarray(resb.results[c]["out_tm"]) for c in cores])


# version 05 (final)
# speedup vs baseline: 1.0198x; 1.0198x over previous
import math
import re
import numpy as np
import ml_dtypes
import concourse.bass as bass
import concourse.mybir as mybir
from concourse.bass_utils import run_bass_kernel_spmd
from contextlib import ExitStack

F32 = mybir.dt.float32
BF16 = mybir.dt.bfloat16
I32 = mybir.dt.int32
AF = mybir.ActivationFunctionType
ALU = mybir.AluOpType
AX = mybir.AxisListType
BF = ml_dtypes.bfloat16

NCORES = 8
T = 4096
D = 1024
NC = D // 128
TT = 512
NT = T // TT
QT = 512
NQ = T // QT
LN_EPS = 1e-5

ENGINES = ("sync", "scalar", "vector", "tensor", "gpsimd")
FIFO_EDGES = False


class Sem:
    __slots__ = ("h", "n", "name")

    def __init__(self, h, name):
        self.h = h
        self.n = 0
        self.name = name


class Ev:
    __slots__ = ("sem", "val", "eng", "dma")

    def __init__(self, sem, val, eng, dma):
        self.sem = sem
        self.val = val
        self.eng = eng
        self.dma = dma


class KB:

    def __init__(self, nc):
        self.nc = nc
        self.ctx = ExitStack()
        self.ops = {e: [] for e in ENGINES}
        self.qsem = {e: None for e in ENGINES}
        self.streams = {}
        self.qstreams = {e: [] for e in ENGINES}
        self.nsem = 0
        self.qlast = {}

    def sem(self, name=None):
        name = name or f"s{self.nsem}"
        self.nsem += 1
        return Sem(self.ctx.enter_context(self.nc.semaphore(name)), name)

    def _q(self, eng):
        if self.qsem[eng] is None:
            self.qsem[eng] = self.sem(f"q_{eng}")
        return self.qsem[eng]

    def sb(self, name, shape, dtype):
        return self.ctx.enter_context(self.nc.sbuf_tensor(name, shape, dtype))

    def ps(self, name, shape, dtype=F32):
        return self.ctx.enter_context(self.nc.psum_tensor(name, shape, dtype))

    def op(self, eng, fn, wait=(), inc=None, amt=1, strict=False, fence=False,
           _dma_sem=None):
        evs = []

        def _flat(w):
            if w is None:
                return
            if isinstance(w, (list, tuple)):
                for x in w:
                    _flat(x)
            elif w.dma:
                evs.append(Ev(w.sem, w.sem.n, w.eng, True))
            else:
                evs.append(w)
        _flat(list(wait) if isinstance(wait, (list, tuple)) else [wait])
        if FIFO_EDGES and _dma_sem is None and self.qlast.get(eng) is not None:
            ql = self.qlast[eng]
            evs.append(Ev(ql.sem, ql.val, "__fifo__", False))
        s = self._q(eng)
        if fence and s.n > 0:
            evs.append(Ev(s, s.n, "__fence__", True))
        if _dma_sem is not None:
            _dma_sem.n += amt
            ev = Ev(_dma_sem, _dma_sem.n, eng, True)
            self.ops[eng].append((fn, evs, (_dma_sem, amt)))
            return ev
        s.n += amt
        ev = Ev(s, s.n, eng, amt >= 16 or strict)
        self.ops[eng].append((fn, evs, (s, amt)))
        self.qlast[eng] = ev
        return ev

    def dma(self, q, out, in_, wait=(), inc=None, stream=None, **kw):
        if stream is None:
            try:
                o_sb = out.space == bass.MemorySpace.SBUF
            except Exception:
                o_sb = False
            nm = out.name if o_sb else "st_" + out.name
            stream = re.sub(r"_[0-9]+$", "", nm)
        key = (q, stream)
        if key not in self.streams:
            self.streams[key] = self.sem(f"d_{key[0]}_{key[1]}"[:30])
            self.qstreams[q].append(self.streams[key])
        return self.op(q, lambda e: e.dma_start(out=out, in_=in_, **kw),
                       wait, None, 16, _dma_sem=self.streams[key])

    def queue_done(self, eng):
        evs = [Ev(s, s.n, "__fin__", True) for s in self.qstreams[eng] if s.n]
        s = self._q(eng)
        if s.n:
            evs.append(Ev(s, s.n, "__fin__", True))
        return evs

    def barrier(self, engines):
        return [self.op(e, lambda en: en.nop()) for e in engines]

    def build(self):
        waitmax = {}

        def mk_body(eng, items):
            def body(e):
                for (fn, evs, inc) in items:
                    for ev in evs:
                        if ev.eng == eng and not ev.dma:
                            continue
                        key = (eng, ev.sem.name)
                        if waitmax.get(key, -1) >= ev.val:
                            continue
                        e.wait_ge(ev.sem.h, ev.val)
                        waitmax[key] = ev.val
                    inst = fn(e)
                    assert inst is not None, "op fn returned None (compound op?)"
                    if inc is not None:
                        inst.then_inc(inc[0].h, inc[1])
            return body

        with self.nc.Block() as block:
            for eng in ENGINES:
                if self.ops[eng]:
                    getattr(block, eng)(mk_body(eng, self.ops[eng]))
        self.ctx.close()



class PostLN:

    def __init__(self, k, name, psum):
        self.k = k
        self.ps = psum
        self.mrow = k.sb(f"{name}_m", [1, TT], F32)
        self.vrow = k.sb(f"{name}_v", [1, TT], F32)
        self.lrow = k.sb(f"{name}_l", [1, TT], F32)
        self.rrow = k.sb(f"{name}_r", [1, TT], F32)
        self.mrrow = k.sb(f"{name}_mr", [1, TT], F32)
        self.rb = k.sb(f"{name}_rb", [128, TT], F32)
        self.mrb = k.sb(f"{name}_mrb", [128, TT], F32)
        self.sqt = [k.sb(f"{name}_sq{i}", [128, TT], F32) for i in range(2)]
        self.xt = [k.sb(f"{name}_x{i}", [128, TT], F32) for i in range(2)]
        self.ones_col = k.sb(f"{name}_oc", [128, 1], F32)
        self.ones_row = k.sb(f"{name}_or", [1, 128], F32)
        self.eps_row = k.sb(f"{name}_eps", [1, 1], F32)
        self.e_oc = k.op("vector", lambda e: e.memset(self.ones_col[:], 1.0))
        self.e_or = k.op("vector", lambda e: e.memset(self.ones_row[:], 1.0))
        self.e_eps = k.op("vector", lambda e: e.memset(self.eps_row[:], LN_EPS))
        self.e_stats_read = None
        self.e_sq_mm_prev = None
        self.e_r_prev = None
        self.e_b1_prev = None
        self.e_b2_prev = None
        self.e_mr_prev = None
        self.war = {}

    def tile(self, zrows, z_evs, sink):
        k = self.k
        e_sum_mm = None
        e_sq_mm = None
        e_sqmm_by = {}
        for dout in range(NC):
            e_sum_mm = k.op("tensor", lambda e, z=zrows[dout], dout=dout: e.matmul(
                self.ps["sum"], self.ones_col[:], z,
                start=(dout == 0), stop=(dout == NC - 1)),
                wait=[z_evs[dout], self.e_oc] +
                     ([self.e_stats_read] if dout == 0 else []))
        for dout in range(NC):
            e_sq = k.op("vector", lambda e, z=zrows[dout], b=self.sqt[dout % 2]:
                        e.tensor_mul(b[:], z, z),
                        wait=[z_evs[dout], e_sqmm_by.get(dout - 2),
                              self.e_sq_mm_prev if dout < 2 else None])
            e_sq_mm = k.op("tensor", lambda e, dout=dout, b=self.sqt[dout % 2]:
                           e.matmul(self.ps["sq"], self.ones_col[:], b[:],
                                    start=(dout == 0), stop=(dout == NC - 1)),
                           wait=[e_sq] + ([self.e_stats_read] if dout == 0 else []))
            e_sqmm_by[dout] = e_sq_mm
        self.e_sq_mm_prev = e_sq_mm
        e_m = k.op("vector", lambda e: e.tensor_scalar_mul(
            self.mrow[:], self.ps["sum"], 1.0 / D),
            wait=[e_sum_mm, self.e_mr_prev])
        e_v1 = k.op("vector", lambda e: e.tensor_scalar_mul(
            self.vrow[:], self.ps["sq"], 1.0 / D),
            wait=[e_sq_mm, self.e_r_prev])
        e_v2 = k.op("vector", lambda e: e.tensor_mul(
            self.lrow[:], self.mrow[:], self.mrow[:]), wait=[self.e_r_prev])
        e_v3 = k.op("vector", lambda e: e.tensor_sub(
            self.vrow[:], self.vrow[:], self.lrow[:]))
        self.e_stats_read = e_v3
        e_l1 = k.op("scalar", lambda e: e.activation(
            self.lrow[:], self.vrow[:], AF.Ln, bias=self.eps_row[:, 0:1]),
            wait=[e_v3, self.e_eps])
        e_r = k.op("scalar", lambda e: e.activation(
            self.rrow[:], self.lrow[:], AF.Exp, scale=-0.5),
            wait=[self.e_mr_prev, self.e_b1_prev])
        self.e_r_prev = e_r
        e_mr = k.op("vector", lambda e: e.tensor_mul(
            self.mrrow[:], self.mrow[:], self.rrow[:]),
            wait=[e_r, self.e_b2_prev])
        self.e_mr_prev = e_mr
        e_b1 = k.op("tensor", lambda e: e.matmul(
            self.ps["b1"], self.ones_row[:], self.rrow[:], start=True, stop=True),
            wait=[e_r, self.e_or, self.e_b1_prev])
        e_rb = k.op("vector", lambda e: e.tensor_copy(self.rb[:], self.ps["b1"]),
                    wait=[e_b1])
        self.e_b1_prev = e_rb
        e_b2 = k.op("tensor", lambda e: e.matmul(
            self.ps["b2"], self.ones_row[:], self.mrrow[:], start=True, stop=True),
            wait=[e_mr, self.e_or, self.e_b2_prev, e_rb])
        e_mrb = k.op("vector", lambda e: e.tensor_copy(self.mrb[:], self.ps["b2"]),
                     wait=[e_b2])
        self.e_b2_prev = e_mrb
        norm_evs = []
        for dout in range(NC):
            xt = self.xt[dout % 2]
            e_n1 = k.op("vector", lambda e, z=zrows[dout], xt=xt: e.tensor_mul(
                xt[:], z, self.rb[:]),
                wait=[e_rb, self.war.get(dout % 2)])
            e_n2 = k.op("vector", lambda e, xt=xt: e.tensor_sub(
                xt[:], xt[:], self.mrb[:]), wait=[e_mrb])
            self.war[dout % 2] = sink(dout, xt, e_n2)
            norm_evs.append(e_n2)
        return e_v3, norm_evs



def emit_A1(k, io):
    nc = k.nc
    dbg = io.get("dbg")

    x_bf = [k.sb(f"xbf{c}", [128, T + 1], BF16) for c in range(NC)]
    ksb = k.sb("ksb", [128, T], F32)
    vsb = k.sb("vsb", [128, T], F32)
    srsb = k.sb("srsb", [128, T], F32)
    w_ek = k.sb("w_ek", [128, QT], F32)
    w_ekv = k.sb("w_ekv", [128, QT], F32)
    w_A = k.sb("w_A", [128, QT + 1], F32)
    w_B = k.sb("w_B", [128, QT + 1], F32)
    w_n = k.sb("w_n", [128, QT], F32)
    w_d2 = k.sb("w_d2", [128, QT], F32)
    rwkv_bf = k.sb("rwkv_bf", [128, T], BF16)
    stA = k.sb("stA", [128, 1], F32)
    stB = k.sb("stB", [128, 1], F32)
    ew_all = k.sb("ew_all", [128, NC], F32)
    u_all = k.sb("u_all", [128, NC], F32)
    tm_all = {p: k.sb(f"tm_{p}", [128, NC], F32) for p in ("k", "v", "r")}
    tm1m = {p: k.sb(f"tm1m_{p}", [128, NC], F32) for p in ("k", "v", "r")}
    wprep = k.sb("wprep", [128, D], F32)
    wstg = [k.sb(f"wstg{i}", [128, D], BF16) for i in range(2)]
    wsl = [k.sb(f"wsl{i}", [128, D], BF16) for i in range(4)]
    rwkv_t = [k.sb(f"rwt{c}", [128, TT], BF16) for c in range(NC)]
    zrow = [k.sb(f"zrow{c}", [128, TT], F32) for c in range(NC)]
    xres = [k.sb(f"xres{i}", [128, TT], F32) for i in range(2)]
    x1tb = [k.sb(f"x1tb{i}", [128, TT], BF16) for i in range(2)]
    ln05_col = k.sb("ln05_col", [128, 1], F32)

    pj = [k.ps(f"pj{i}", [128, TT], F32) for i in range(8)]
    ln = PostLN(k, "ln1", dict(sum=pj[2][0:1, :], sq=pj[3][0:1, :],
                               b1=pj[4][:], b2=pj[5][:]))

    s_ld = k.sem("ld")
    s_st = k.sem("st")
    s_mm = k.sem("mm")
    s_ev = k.sem("ev")
    s_wk = k.sem("wk")
    s_ms = k.sem("ms")

    e_tm = {p: k.dma("sync", tm_all[p][:], io["tm"][p], inc=s_ld)
            for p in ("k", "v", "r")}
    e_td = k.dma("sync", ew_all[:], io["tdec"], inc=s_ld)
    e_tf = k.dma("sync", u_all[:], io["tfir"], inc=s_ld)
    e_ew = k.op("scalar", lambda e: e.activation(w_ek[:, 0:NC], ew_all[:], AF.Exp),
                wait=[e_td], inc=s_ms)
    e_ew = k.op("scalar", lambda e: e.activation(ew_all[:], w_ek[:, 0:NC], AF.Exp,
                                                 scale=-1.0),
                wait=[e_ew], inc=s_ms)
    e_1m = {p: k.op("vector", lambda e, p=p: e.tensor_scalar(
        tm1m[p][:], tm_all[p][:], -1.0, 1.0, ALU.mult, ALU.add),
        wait=[e_tm[p]], inc=s_ms) for p in ("k", "v", "r")}
    e_l5 = k.op("vector", lambda e: e.memset(ln05_col[:], math.log(0.5)), inc=s_ms)

    e_xbf = []
    for c in range(NC):
        ez = k.op("vector", lambda e, c=c: e.memset(x_bf[c][:, 0:1], 0.0), inc=s_ms)
        e_xbf.append(k.dma("gpsimd", x_bf[c][:, 1:T + 1],
                           io["x_cm"][c * 128:(c + 1) * 128, :],
                           wait=[ez], inc=s_ld))

    e_sc_store = {}
    e_prev_stg = [None, None]
    e_prev_prep = None
    for p in ("k", "v", "r"):
        for c in range(NC):
            e_in = k.dma("sync", wprep[:], io["wT"][p][c * 128:(c + 1) * 128, :],
                         wait=[e_prev_prep], inc=s_ld)
            ea = k.op("vector", lambda e, p=p, c=c: e.tensor_scalar_mul(
                wstg[0][:], wprep[:], tm_all[p][:, c:c + 1]),
                wait=[e_in, e_tm[p], e_prev_stg[0]], inc=s_ev)
            eb = k.op("vector", lambda e, p=p, c=c: e.tensor_scalar_mul(
                wstg[1][:], wprep[:], tm1m[p][:, c:c + 1]),
                wait=[e_in, e_1m[p], e_prev_stg[1]], inc=s_ev)
            e_prev_prep = eb
            outs = []
            for abi, ab in enumerate(("a", "b")):
                dst = io["wsc5"][p + ab].rearrange("(d prt c j) -> prt d c j",
                                                   d=NC, prt=128, c=NC)[:, :, c, :]
                outs.append(k.dma("sync", dst,
                                  wstg[abi][:].rearrange("p (d j) -> p d j", d=NC),
                                  wait=[ea if abi == 0 else eb], inc=s_st))
            e_prev_stg = outs
            e_sc_store[(p, c)] = outs
    e_wo_store = {}
    for c in range(NC):
        e_in = k.dma("gpsimd", wstg[0][:], io["woT"][c * 128:(c + 1) * 128, :],
                     wait=[e_prev_stg[0], e_prev_prep], inc=s_ld)
        dst = io["wo5"].rearrange("(d prt c j) -> prt d c j",
                                  d=NC, prt=128, c=NC)[:, :, c, :]
        eo = k.dma("sync", dst, wstg[0][:].rearrange("p (d j) -> p d j", d=NC),
                   wait=[e_in], inc=s_st)
        e_prev_stg[0] = eo
        e_prev_prep = None
        e_wo_store[c] = eo
    e_wo_all = [e_wo_store[c] for c in range(NC)]

    bank_free = [None] * 8
    free_k = [None] * NQ
    free_v = [None] * NQ
    free_sr = [None] * NQ
    free_rw = None
    e_slab_use = [None] * 4
    kv_ready = {}
    sr_ready = {}
    e_rwkv_store = [None] * NC

    for d in range(NC):
        kv_ready = {q: [] for q in range(NQ)}
        sr_ready = {q: [] for q in range(NQ)}
        for pi, p in enumerate(("k", "v", "r")):
            for half in range(2):
                tset = list(range(half * 4, half * 4 + 4))
                e_mm = {t: None for t in tset}
                for abi, ab in enumerate(("a", "b")):
                    sl = wsl[2 * half + abi]
                    src = io["wsc5"][p + ab][d * 128 * D:(d + 1) * 128 * D]
                    src = src.rearrange("(prt c j) -> prt (c j)", prt=128, c=NC)
                    e_w = k.dma("sync", sl[:], src,
                                wait=[e_slab_use[2 * half + abi],
                                      [e_sc_store[(p, c)][abi] for c in range(NC)]],
                                inc=s_ld)
                    for c in range(NC):
                        lhs = sl[:, c * 128:(c + 1) * 128]
                        for t in tset:
                            off = 1 - abi
                            rhs = x_bf[c][:, off + t * TT: off + t * TT + TT]
                            first = (abi == 0 and c == 0)
                            last = (abi == 1 and c == NC - 1)
                            waits = [e_w, e_xbf[c]]
                            if first:
                                waits.append(bank_free[t])
                                q = (t * TT) // QT
                                if p == "k":
                                    waits.append(free_k[q])
                                elif p == "v":
                                    waits.append(free_v[q])
                                else:
                                    waits.append(free_sr[q])
                            e_mm[t] = k.op("tensor",
                                           lambda e, lhs=lhs, rhs=rhs, b=pj[t],
                                           first=first, last=last:
                                           e.matmul(b[:], lhs, rhs,
                                                    start=first, stop=last),
                                           wait=waits, inc=s_mm)
                    e_slab_use[2 * half + abi] = e_mm[tset[-1]]
                for t in tset:
                    q = (t * TT) // QT
                    if p == "r":
                        ee = k.op("scalar", lambda e, t=t: e.activation(
                            srsb[:, t * TT:(t + 1) * TT], pj[t][:], AF.Tanh,
                            scale=0.5),
                            wait=[e_mm[t]], inc=s_ev)
                        sr_ready[q].append(ee)
                    else:
                        dst = ksb if p == "k" else vsb
                        ee = k.op("vector", lambda e, t=t, dst=dst: e.tensor_copy(
                            dst[:, t * TT:(t + 1) * TT], pj[t][:]),
                            wait=[e_mm[t]], inc=s_ev)
                        kv_ready[q].append(ee)
                    bank_free[t] = ee

        ew_b = ew_all[:, d:d + 1].broadcast_to([128, QT])
        u_col = u_all[:, d:d + 1]
        e_seg_last = None
        for q in range(NQ):
            t0 = q * QT
            kk = ksb[:, t0:t0 + QT]
            vv = vsb[:, t0:t0 + QT]
            sr = srsb[:, t0:t0 + QT]
            initA = 0.0 if q == 0 else stA[:, 0:1]
            initB = 0.0 if q == 0 else stB[:, 0:1]
            if q == 0:
                k.op("vector", lambda e: e.memset(w_A[:, 0:1], 0.0), inc=s_wk)
                k.op("vector", lambda e: e.memset(w_B[:, 0:1], 0.0), inc=s_wk)
            else:
                k.op("vector", lambda e: e.tensor_copy(w_A[:, 0:1], stA[:]),
                     inc=s_wk)
                k.op("vector", lambda e: e.tensor_copy(w_B[:, 0:1], stB[:]),
                     inc=s_wk)
            e1 = k.op("scalar", lambda e, kk=kk: e.activation(w_ek[:], kk, AF.Exp),
                      wait=[kv_ready[q], e_seg_last], inc=s_wk)
            e2 = k.op("vector", lambda e, vv=vv: e.tensor_mul(w_ekv[:], w_ek[:], vv),
                      wait=[e1, kv_ready[q]], inc=s_wk)
            e3 = k.op("vector", lambda e, initA=initA, ew_b=ew_b: e.tensor_tensor_scan(
                w_A[:, 1:QT + 1], ew_b, w_ekv[:], initA, ALU.mult, ALU.add),
                wait=[e2, e_ew], inc=s_wk)
            e4 = k.op("vector", lambda e, initB=initB, ew_b=ew_b: e.tensor_tensor_scan(
                w_B[:, 1:QT + 1], ew_b, w_ek[:], initB, ALU.mult, ALU.add),
                wait=[e1], inc=s_wk)
            if dbg is not None and d == 0 and q == 0:
                eda = k.dma("gpsimd", dbg["wA"], w_A[:], wait=[e3], inc=s_st)
                edb = k.dma("gpsimd", dbg["wB"], w_B[:], wait=[e4], inc=s_st)
                k.op("vector", lambda e: e.nop(), wait=[eda, edb], inc=s_wk)
            e5 = k.op("vector", lambda e: e.tensor_copy(stA[:], w_A[:, QT:QT + 1]),
                      inc=s_wk)
            e6 = k.op("vector", lambda e: e.tensor_copy(stB[:], w_B[:, QT:QT + 1]),
                      inc=s_wk)
            e7 = k.op("scalar", lambda e, kk=kk, u_col=u_col: e.activation(
                w_ek[:], kk, AF.Exp, bias=u_col), wait=[e4, e_tf], inc=s_wk)
            free_k[q] = e7
            e8 = k.op("vector", lambda e, vv=vv: e.tensor_mul(w_n[:], w_ek[:], vv),
                      wait=[e7], inc=s_wk)
            free_v[q] = e8
            e9 = k.op("vector", lambda e: e.tensor_add(w_n[:], w_n[:], w_A[:, 0:QT]),
                      inc=s_wk)
            e10 = k.op("vector", lambda e: e.tensor_add(w_d2[:], w_ek[:], w_B[:, 0:QT]),
                       wait=[e7], inc=s_wk)
            e11 = k.op("scalar", lambda e: e.activation(w_ekv[:], w_d2[:], AF.Ln),
                       wait=[e10, e3], inc=s_wk)
            e12 = k.op("scalar", lambda e: e.activation(
                w_d2[:], w_ekv[:], AF.Exp, scale=-1.0, bias=ln05_col[:, 0:1]),
                wait=[e11, e_l5], inc=s_wk)
            e13 = k.op("vector", lambda e: e.tensor_mul(w_n[:], w_n[:], w_d2[:]),
                       wait=[e9, e12], inc=s_wk)
            e14 = k.op("vector", lambda e, sr=sr, t0=t0: e.scalar_tensor_tensor(
                rwkv_bf[:, t0:t0 + QT], sr, 1.0, w_n[:], ALU.add, ALU.mult),
                wait=[e13, sr_ready[q], free_rw], inc=s_wk)
            free_sr[q] = e14
            e_seg_last = e14
        free_rw = k.dma("sync", io["rwkv_d"][d * 128:(d + 1) * 128, :], rwkv_bf[:],
                        wait=[e_seg_last], inc=s_st)
        e_rwkv_store[d] = free_rw
        if dbg is not None and d == 0:
            k.dma("gpsimd", dbg["ew"], ew_all[:], wait=[e_seg_last], inc=s_st)
            k.dma("gpsimd", dbg["k0"], ksb[:], wait=[e_seg_last], inc=s_st)
            k.dma("gpsimd", dbg["v0"], vsb[:], wait=[e_seg_last], inc=s_st)
            k.dma("gpsimd", dbg["sr0"], srsb[:], wait=[e_seg_last], inc=s_st)

    e_bar = k.barrier(["tensor", "vector", "scalar"])

    def prev2(d_, t_):
        return (t_, d_ - 2) if d_ >= 2 else (t_ - 1, d_ + NC - 2)

    e_rw_use = {}
    e_slab_use4 = [e_bar, e_bar]
    e_stats_read = None
    e_zadd = {}
    e_x1_dma = {}
    for t in range(NT):
        e_rw = {}
        for c in range(NC):
            e_rw[c] = k.dma("scalar", rwkv_t[c][:],
                            io["rwkv_d"][c * 128:(c + 1) * 128, t * TT:(t + 1) * TT],
                            wait=[e_rwkv_store[c], e_rw_use.get(c), e_bar],
                            inc=s_ld)
        e_sum_mm = None
        e_sq_mm = None
        for dout in range(NC):
            bank = pj[dout % 2]
            sl = wsl[dout % 2]
            src = io["wo5"][dout * 128 * D:(dout + 1) * 128 * D]
            src = src.rearrange("(prt c j) -> prt (c j)", prt=128, c=NC)
            e_w = k.dma("sync", sl[:], src,
                        wait=[e_wo_all, e_slab_use4[dout % 2]], inc=s_ld)
            e_mm = None
            for c in range(NC):
                e_mm = k.op("tensor", lambda e, sl=sl, c=c, bank=bank:
                            e.matmul(bank[:], sl[:, c * 128:(c + 1) * 128],
                                     rwkv_t[c][:],
                                     start=(c == 0), stop=(c == NC - 1)),
                            wait=[e_w, e_rw[c]] +
                                 ([e_zadd.get(prev2(dout, t))] if c == 0 else []),
                            inc=s_mm)
                e_rw_use[c] = e_mm
            e_slab_use4[dout % 2] = e_mm
            xr = xres[dout % 2]
            e_x = k.dma("scalar", xr[:],
                        io["x_cm"][dout * 128:(dout + 1) * 128, t * TT:(t + 1) * TT],
                        wait=[e_zadd.get(prev2(dout, t)), e_bar], inc=s_ld)
            e_z = k.op("vector", lambda e, dout=dout, bank=bank, xr=xr: e.tensor_add(
                zrow[dout][:], bank[:], xr[:]),
                wait=[e_mm, e_x, e_stats_read], inc=s_ev)
            e_zadd[(t, dout)] = e_z
        def sink(dout, xt, ev, t=t):
            xtb = x1tb[dout % 2]
            e_n3 = k.op("vector", lambda e, xt=xt, xtb=xtb: e.tensor_copy(
                xtb[:], xt[:]),
                wait=[e_x1_dma.get(prev2(dout, t) + (1,))])
            o1 = k.dma("sync", io["x1f"][dout * 128:(dout + 1) * 128,
                                         t * TT:(t + 1) * TT],
                       xt[:], wait=[ev])
            o2 = k.dma("sync", io["x1b"][dout * 128:(dout + 1) * 128,
                                         t * TT:(t + 1) * TT],
                       xtb[:], wait=[e_n3])
            e_x1_dma[(t, dout, 0)] = o1
            e_x1_dma[(t, dout, 1)] = o2
            return [o1, e_n3]

        e_stats_read, _ = ln.tile([zrow[d_][:] for d_ in range(NC)],
                                  [e_zadd[(t, d_)] for d_ in range(NC)], sink)

    k.op("vector", lambda e: e.engine_nop(),
         wait=[k.queue_done("sync"), k.queue_done("gpsimd"),
               k.queue_done("scalar")])


def build_A1(dbg=False):
    nc = bass.Bass("TRN2", target_bir_lowering=False)
    io = {}
    if dbg:
        io["dbg"] = {
            "k0": nc.dram_tensor("dbg_k0", [128, T], F32, kind="ExternalOutput").ap(),
            "ew": nc.dram_tensor("dbg_ew", [128, NC], F32, kind="ExternalOutput").ap(),
            "wA": nc.dram_tensor("dbg_wA", [128, QT + 1], F32, kind="ExternalOutput").ap(),
            "wB": nc.dram_tensor("dbg_wB", [128, QT + 1], F32, kind="ExternalOutput").ap(),
            "v0": nc.dram_tensor("dbg_v0", [128, T], F32, kind="ExternalOutput").ap(),
            "sr0": nc.dram_tensor("dbg_sr0", [128, T], F32, kind="ExternalOutput").ap(),
        }
    io["x_cm"] = nc.dram_tensor("x_cm", [D, T], F32, kind="ExternalInput").ap()
    io["wT"] = {p: nc.dram_tensor(f"w{p}T", [D, D], F32, kind="ExternalInput").ap()
                for p in ("k", "v", "r")}
    io["woT"] = nc.dram_tensor("woT", [D, D], F32, kind="ExternalInput").ap()
    for nm in ("tmk", "tmv", "tmr"):
        pass
    io["tm"] = {p: nc.dram_tensor(f"tm{p}", [128, NC], F32, kind="ExternalInput").ap()
                for p in ("k", "v", "r")}
    io["tdec"] = nc.dram_tensor("tdec", [128, NC], F32, kind="ExternalInput").ap()
    io["tfir"] = nc.dram_tensor("tfir", [128, NC], F32, kind="ExternalInput").ap()
    io["x1f"] = nc.dram_tensor("x1f", [D, T], F32, kind="ExternalOutput").ap()
    io["x1b"] = nc.dram_tensor("x1b", [D, T], BF16, kind="ExternalOutput").ap()
    io["wsc5"] = {p + ab: nc.dram_tensor(f"wsc_{p}{ab}", [NC * 128 * NC * 128], BF16,
                                         kind="Internal").ap()
                  for p in ("k", "v", "r") for ab in ("a", "b")}
    io["wo5"] = nc.dram_tensor("wo5", [NC * 128 * NC * 128], BF16, kind="Internal").ap()
    io["rwkv_d"] = nc.dram_tensor("rwkv_d", [D, T], BF16,
                                  kind="ExternalOutput" if dbg else "Internal").ap()

    k = KB(nc)
    emit_A1(k, io)
    k.build()
    return nc



NE = 15
BLK = 1024
NBLK = T // BLK
NSUB = BLK // TT


def emit_A2(k, io):
    nc = k.nc
    dbg = io.get("dbg")
    dbg_evs = []

    x1blk = [[k.sb(f"x1b{i}_{c}", [128, BLK], BF16) for c in range(NC)]
             for i in range(2)]
    w1b = [[k.sb(f"w1b{i}_{c}", [128, D], BF16) for c in range(NC)]
           for i in range(2)]
    w2b = [[k.sb(f"w2b{i}_{c}", [128, D], BF16) for c in range(NC)]
           for i in range(2)]
    hsc = [k.sb(f"hsc{h}", [128, BLK], BF16) for h in range(NC)]
    routed = [k.sb(f"rout{d}", [128, BLK], F32) for d in range(NC)]
    tmp = [k.sb(f"tmpe{i}", [128, TT], F32) for i in range(2)]
    cw_b = k.sb("cw_b", [128, TT], F32)
    xres = [k.sb(f"xres{i}", [128, TT], F32) for i in range(2)]
    x2tb = [k.sb(f"x2tb{i}", [128, TT], BF16) for i in range(2)]
    rwb = k.sb("rwb", [128, NE * NC], BF16)
    rbb = k.sb("rbb", [128, 16], F32)
    rb_row = k.sb("rb_row", [1, NE], F32)
    lg = k.sb("lg", [128, 16], F32)
    exs = k.sb("exs", [128, 16], F32)
    nmx = k.sb("nmx", [128, 1], F32)
    ssum = k.sb("ssum", [128, 1], F32)
    ssum2 = k.sb("ssum2", [128, 1], F32)
    probs = k.sb("probs", [128, 16], F32)
    mw8 = k.sb("mw8", [128, 8], F32)
    maskf = k.sb("maskf", [128, 16], F32)
    cwf = k.sb("cwf", [128, 16], F32)
    cw_pad = k.sb("cw_pad", [128, 128], BF16)
    ctp = k.sb("ctp", [128, 128], BF16)
    cwT = k.sb("cwT", [15, BLK], BF16)
    sb1p = k.sb("sb1t", [128, NC], F32)
    sb2p = k.sb("sb2t", [128, NC], F32)
    eb1p = k.sb("eb1t", [128, NE * NC], F32)
    eb2b = k.sb("eb2b", [15, D], BF16)
    ohb = k.sb("ohb", [15, NE * 128], BF16)

    pj = [k.ps(f"pj{i}", [128, TT], F32) for i in range(8)]
    ln = PostLN(k, "ln2", dict(sum=pj[5][0:1, :], sq=pj[6][0:1, :],
                               b1=pj[7][:], b2=pj[7][:]))
    lgps = pj[4][:, 0:16]

    e_rw = []
    for c in range(NC):
        e_rw.append(k.dma("gpsimd", rwb[:, c * NE:(c + 1) * NE],
                          io["router_W"][c * 128:(c + 1) * 128, :]))
    e_rbrow = k.dma("sync", rb_row[:], io["router_b"])
    e_sb1 = k.dma("sync", sb1p[:], io["sb1p"])
    e_sb2 = k.dma("sync", sb2p[:], io["sb2p"])
    e_eb1 = k.dma("sync", eb1p[:], io["eb1p"])
    e_eb2 = k.dma("gpsimd", eb2b[:], io["eb2"])
    e_oh = k.dma("sync", ohb[:], io["oh"])
    e_cwz = k.op("vector", lambda e: e.memset(cw_pad[:], 0.0))
    e_rbb = k.op("tensor", lambda e: e.matmul(lgps[:, 0:NE], ln.ones_row[:],
                                              rb_row[:], start=True, stop=True),
                 wait=[e_rbrow, ln.e_or])
    e_rbb = k.op("vector", lambda e: e.tensor_copy(rbb[:, 0:NE], lgps[:, 0:NE]),
                 wait=[e_rbb])
    bank4_free = e_rbb

    e_xblk_use = [None, None]
    e_wuse = [None, None]
    e_l2last = {}
    e_relu = {}
    e_tmp_prev = [None, None]
    e_bankL1 = [None, None]
    e_bankL2 = [None, None]
    e_routed_war = None
    e_x2_dma = {}
    e_xres_prev = [None, None]
    e_cwT_ready = None
    e_cwT_read = None
    e_cwb_read = None
    e_ctp_use = None

    def prev2k(d_, blk_):
        return (blk_, d_ - 2) if d_ >= 2 else (blk_ - 1, d_ + NC - 2)

    for blk in range(NBLK):
        bpar = blk % 2
        xb = x1blk[bpar]
        e_xb = []
        for c in range(NC):
            e_xb.append(k.dma("sync", xb[c][:],
                              io["x1b"][c * 128:(c + 1) * 128,
                                        blk * BLK:(blk + 1) * BLK],
                              wait=[e_xblk_use[bpar]]))
        e_cwT = []
        e3_prev = None
        e_e12_prev = None
        for ts in range(NC):
            e_lg = None
            for c in range(NC):
                e_lg = k.op("tensor", lambda e, c=c, ts=ts, xb=xb: e.matmul(
                    lgps[:, 0:NE], xb[c][:, ts * 128:(ts + 1) * 128],
                    rwb[:, c * NE:(c + 1) * NE],
                    start=(c == 0), stop=(c == NC - 1)),
                    wait=[e_xb[c], e_rw[c]] + ([bank4_free] if c == 0 else []) +
                         (dbg_evs if dbg is not None and blk == 0 and ts == 1
                          and c == 0 else []))
            e1 = k.op("vector", lambda e: e.tensor_add(lg[:, 0:NE], lgps[:, 0:NE],
                                                       rbb[:, 0:NE]),
                      wait=[e_lg, e_rbb, e3_prev])
            bank4_free = e1
            e1b = k.op("vector", lambda e: e.memset(lg[:, NE:16], -1e38), wait=[])
            e2 = k.op("vector", lambda e: e.reduce_max(nmx[:], lg[:], axis=AX.X),
                      wait=[e3_prev], strict=True, fence=True)
            e2b = k.op("vector", lambda e: e.tensor_scalar_mul(nmx[:], nmx[:], -1.0),
                      wait=[e2])
            e3 = k.op("scalar", lambda e: e.activation(exs[:], lg[:], AF.Exp,
                                                       bias=nmx[:, 0:1]),
                      wait=[e2b, e1])
            e4 = k.op("vector", lambda e: e.reduce_sum(ssum[:], exs[:, 0:NE],
                                                       axis=AX.X), wait=[e3],
                      strict=True, fence=True)
            e3_prev = e3
            e5a = k.op("scalar", lambda e: e.activation(ssum2[:], ssum[:], AF.Ln),
                       wait=[e4])
            e5 = k.op("scalar", lambda e: e.activation(ssum2[:], ssum2[:], AF.Exp,
                                                       scale=-1.0))
            e6 = k.op("vector", lambda e: e.tensor_scalar_mul(probs[:, 0:NE],
                                                              exs[:, 0:NE],
                                                              ssum2[:, 0:1]),
                      wait=[e3, e5] + (dbg_evs if dbg is not None and blk == 0
                                       and ts == 1 else []))
            e6b = k.op("vector", lambda e: e.memset(probs[:, NE:16], -1e38))
            e7 = k.op("vector", lambda e: e.max(mw8[:], probs[:]),
                      wait=[e6b], strict=True, fence=True)
            e8 = k.op("vector", lambda e: e.tensor_single_scalar(
                maskf[:, 0:NE], probs[:, 0:NE], mw8[:, 2:3], ALU.is_ge),
                wait=[e7], strict=True, fence=True)
            e9 = k.op("vector", lambda e: e.tensor_mul(cwf[:, 0:NE],
                                                       maskf[:, 0:NE],
                                                       probs[:, 0:NE]),
                      wait=[e8], strict=True, fence=True)
            if dbg is not None and blk == 0 and ts == 0:
                dbg_gate = [k.dma("gpsimd", dbg["probs0"], probs[:], wait=[e9]),
                            k.dma("gpsimd", dbg["mw80"], mw8[:], wait=[e9]),
                            k.dma("gpsimd", dbg["lg0"], lg[:], wait=[e9]),
                            k.dma("gpsimd", dbg["rbb0"], rbb[:], wait=[e9]),
                            k.dma("gpsimd", dbg["mask0"], cwf[:], wait=[e9])]
                dbg_evs.extend(dbg_gate)
            elif dbg is not None and blk == 0 and ts == 1:
                pass
            e10 = k.op("vector", lambda e, ts=ts: e.tensor_copy(
                cw_pad[:, 0:NE], cwf[:, 0:NE]),
                wait=[e_cwz, e_ctp_use], strict=True, fence=True)
            e11 = k.dma("scalar", ctp[:], cw_pad[:], transpose=True,
                        wait=[e10, e_e12_prev])
            e_ctp_use = e11
            e12 = k.op("vector", lambda e, ts=ts: e.tensor_copy(
                cwT[:, ts * 128:ts * 128 + 128], ctp[0:NE, :]),
                wait=[e11, e_cwT_read])
            e_e12_prev = e12
            e_cwT.append(e12)
        e_cwT_all = list(e_cwT)
        if dbg is not None and blk == 0:
            dbg_evs.append(k.dma("gpsimd", dbg["cwT0"], cwT[:],
                                 wait=[e_cwT_all]))

        for ei in range(-1, NE):
            wpar = (ei + 1) % 2
            w1 = w1b[wpar]
            w2 = w2b[wpar]
            e_w1 = []
            e_w2 = []
            if ei < 0:
                for c in range(NC):
                    e_w1.append(k.dma("gpsimd", w1[c][:],
                                      io["sW1"][c * 128:(c + 1) * 128, :],
                                      wait=[e_wuse[wpar]]))
                    e_w2.append(k.dma("gpsimd", w2[c][:],
                                      io["sW2"][c * 128:(c + 1) * 128, :],
                                      wait=[e_wuse[wpar]]))
            else:
                for c in range(NC):
                    e_w1.append(k.dma("gpsimd", w1[c][:],
                                      io["eW1"][(ei * D + c * 128) * D:
                                                (ei * D + c * 128 + 128) * D]
                                      .rearrange("(p j) -> p j", p=128),
                                      wait=[e_wuse[wpar]]))
                    e_w2.append(k.dma("gpsimd", w2[c][:],
                                      io["eW2"][(ei * D + c * 128) * D:
                                                (ei * D + c * 128 + 128) * D]
                                      .rearrange("(p j) -> p j", p=128),
                                      wait=[e_wuse[wpar]]))
            for st in range(NSUB):
                tsl = slice(st * TT, (st + 1) * TT)
                if ei >= 0:
                    e_cb = k.op("tensor", lambda e, ei=ei, st=st: e.matmul(
                        pj[4][:], ohb[:, ei * 128:(ei + 1) * 128],
                        cwT[:, st * TT:(st + 1) * TT],
                        start=True, stop=True),
                        wait=[e_cwT_all, e_oh, bank4_free])
                    e_cwT_read = e_cb
                    e_cbc = k.op("vector", lambda e: e.tensor_copy(cw_b[:], pj[4][:]),
                                 wait=[e_cb, e_cwb_read])
                    bank4_free = e_cbc
                for hd in range(NC):
                    bank = pj[hd % 2]
                    e_mm = None
                    for c in range(NC):
                        e_mm = k.op("tensor", lambda e, c=c, hd=hd, xb=xb, w1=w1,
                                    bank=bank, tsl=tsl: e.matmul(
                                        bank[:], w1[c][:, hd * 128:(hd + 1) * 128],
                                        xb[c][:, tsl], start=(c == 0),
                                        stop=(c == NC - 1)),
                                    wait=[e_w1[c], e_xb[c]] +
                                         ([e_bankL1[hd % 2]] if c == 0 else []))
                    if ei < 0:
                        e_ev = k.op("scalar", lambda e, hd=hd, bank=bank, st=st:
                                    e.activation(hsc[hd][:, st * TT:(st + 1) * TT],
                                                 bank[:], AF.Relu,
                                                 bias=sb1p[:, hd:hd + 1]),
                                    wait=[e_mm, e_sb1, e_l2last.get(st)])
                    else:
                        e_st = k.op("vector", lambda e, hd=hd, bank=bank, ei=ei,
                                    b=tmp[hd % 2]: e.scalar_tensor_tensor(
                                        b[:], bank[:],
                                        eb1p[:, ei * NC + hd:ei * NC + hd + 1],
                                        cw_b[:], ALU.add, ALU.mult),
                                    wait=[e_mm, e_eb1, e_cbc,
                                          e_tmp_prev[hd % 2]])
                        e_cwb_read = e_st
                        e_ev = k.op("scalar", lambda e, hd=hd, st=st, b=tmp[hd % 2]:
                                    e.activation(hsc[hd][:, st * TT:(st + 1) * TT],
                                                 b[:], AF.Relu),
                                    wait=[e_st, e_l2last.get(st)])
                        e_tmp_prev[hd % 2] = e_ev
                    e_relu[(st, hd)] = e_ev
                    e_bankL1[hd % 2] = e_ev
                for d2 in range(NC):
                    bank = pj[2 + d2 % 2]
                    e_mm = None
                    for hd in range(NC):
                        e_mm = k.op("tensor", lambda e, hd=hd, d2=d2, w2=w2,
                                    bank=bank, st=st: e.matmul(
                                        bank[:], w2[hd][:, d2 * 128:(d2 + 1) * 128],
                                        hsc[hd][:, st * TT:(st + 1) * TT],
                                        start=(hd == 0), stop=(hd == NC - 1)),
                                    wait=[e_w2[hd], e_relu[(st, hd)]] +
                                         ([e_bankL2[d2 % 2]] if hd == 0 else []))
                    e_l2last[st] = e_mm
                    if ei < 0:
                        e_ac = k.op("vector", lambda e, d2=d2, bank=bank, tsl=tsl:
                                    e.tensor_scalar(routed[d2][:, tsl], bank[:],
                                                    sb2p[:, d2:d2 + 1], None,
                                                    ALU.add),
                                    wait=[e_mm, e_sb2, e_routed_war])
                    else:
                        e_ac = k.op("vector", lambda e, d2=d2, bank=bank, tsl=tsl:
                                    e.tensor_add(routed[d2][:, tsl],
                                                 routed[d2][:, tsl], bank[:]),
                                    wait=[e_mm])
                    e_bankL2[d2 % 2] = e_ac
            e_wuse[wpar] = e_l2last[NSUB - 1]
        e_xblk_use[bpar] = e_l2last[NSUB - 1]

        for st in range(NSUB):
            tsl = slice(st * TT, (st + 1) * TT)
            e_zadds = []
            for d2 in range(NC):
                e_eb = k.op("tensor", lambda e, d2=d2, st=st: e.matmul(
                    pj[4][:], eb2b[:, d2 * 128:(d2 + 1) * 128],
                    cwT[:, st * TT:(st + 1) * TT], start=True, stop=True),
                    wait=[e_eb2, e_cwT_all, bank4_free])
                e_cwT_read = e_eb
                e_ad = k.op("vector", lambda e, d2=d2, tsl=tsl: e.tensor_add(
                    routed[d2][:, tsl], routed[d2][:, tsl], pj[4][:]),
                    wait=[e_eb])
                bank4_free = e_ad
                xr = xres[d2 % 2]
                e_x = k.dma("scalar", xr[:],
                            io["x1f"][d2 * 128:(d2 + 1) * 128,
                                      blk * BLK + st * TT:blk * BLK + (st + 1) * TT],
                            wait=[e_xres_prev[d2 % 2]])
                e_z = k.op("vector", lambda e, d2=d2, tsl=tsl, xr=xr: e.tensor_add(
                    routed[d2][:, tsl], routed[d2][:, tsl], xr[:]),
                    wait=[e_x])
                e_xres_prev[d2 % 2] = e_z
                e_zadds.append(e_z)

            tglob = blk * NSUB + st
            if dbg is not None and tglob == 0:
                for d2 in range(NC):
                    dbg_evs.append(k.dma("gpsimd",
                                         dbg["z0"][d2 * 128:(d2 + 1) * 128, :],
                                         routed[d2][:, 0:TT], wait=[e_zadds[d2]]))

            def sink(dout, xt, ev, tglob=tglob):
                xtb = x2tb[dout % 2]
                e_n3 = k.op("vector", lambda e, xt=xt, xtb=xtb: e.tensor_copy(
                    xtb[:], xt[:]),
                    wait=[e_x2_dma.get((tglob - (1 if dout < 2 else 0),
                                        (dout - 2) % NC, 1))])
                o1 = k.dma("sync", io["x2f"][dout * 128:(dout + 1) * 128,
                                             tglob * TT:(tglob + 1) * TT],
                           xt[:], wait=[ev])
                o2 = k.dma("sync", io["x2b"][dout * 128:(dout + 1) * 128,
                                             tglob * TT:(tglob + 1) * TT],
                           xtb[:], wait=[e_n3])
                e_x2_dma[(tglob, dout, 0)] = o1
                e_x2_dma[(tglob, dout, 1)] = o2
                return [o1, e_n3]

            e_sr, _ = ln.tile([routed[d_][:, tsl] for d_ in range(NC)],
                              e_zadds, sink)
            e_routed_war = e_sr

    k.op("vector", lambda e: e.engine_nop(),
         wait=[k.queue_done("sync"), k.queue_done("gpsimd"),
               k.queue_done("scalar")])


def build_A2(dbg=False):
    nc = bass.Bass("TRN2", target_bir_lowering=False)
    io = {}
    if dbg:
        io["dbg"] = {
            "cwT0": nc.dram_tensor("dbg_cwT0", [NE, BLK], BF16,
                                   kind="ExternalOutput").ap(),
            "z0": nc.dram_tensor("dbg_z0", [D, TT], F32,
                                 kind="ExternalOutput").ap(),
            "probs0": nc.dram_tensor("dbg_probs0", [128, 16], F32,
                                     kind="ExternalOutput").ap(),
            "mw80": nc.dram_tensor("dbg_mw80", [128, 8], F32,
                                   kind="ExternalOutput").ap(),
            "mask0": nc.dram_tensor("dbg_mask0", [128, 16], F32,
                                    kind="ExternalOutput").ap(),
            "lg0": nc.dram_tensor("dbg_lg0", [128, 16], F32,
                                  kind="ExternalOutput").ap(),
            "rbb0": nc.dram_tensor("dbg_rbb0", [128, 16], F32,
                                   kind="ExternalOutput").ap(),
        }
    io["x1f"] = nc.dram_tensor("x1f", [D, T], F32, kind="ExternalInput").ap()
    io["x1b"] = nc.dram_tensor("x1b", [D, T], BF16, kind="ExternalInput").ap()
    io["router_W"] = nc.dram_tensor("router_W", [D, NE], F32, kind="ExternalInput").ap()
    io["router_b"] = nc.dram_tensor("router_b", [1, NE], F32, kind="ExternalInput").ap()
    io["sW1"] = nc.dram_tensor("sW1", [D, D], F32, kind="ExternalInput").ap()
    io["sW2"] = nc.dram_tensor("sW2", [D, D], F32, kind="ExternalInput").ap()
    io["eW1"] = nc.dram_tensor("eW1", [NE * D * D], F32, kind="ExternalInput").ap()
    io["eW2"] = nc.dram_tensor("eW2", [NE * D * D], F32, kind="ExternalInput").ap()
    io["sb1p"] = nc.dram_tensor("sb1p", [128, NC], F32, kind="ExternalInput").ap()
    io["sb2p"] = nc.dram_tensor("sb2p", [128, NC], F32, kind="ExternalInput").ap()
    io["eb1p"] = nc.dram_tensor("eb1p", [128, NE * NC], F32, kind="ExternalInput").ap()
    io["eb2"] = nc.dram_tensor("eb2", [NE, D], F32, kind="ExternalInput").ap()
    io["oh"] = nc.dram_tensor("oh", [NE, NE * 128], BF16, kind="ExternalInput").ap()
    io["x2f"] = nc.dram_tensor("x2f", [D, T], F32, kind="ExternalOutput").ap()
    io["x2b"] = nc.dram_tensor("x2b", [D, T], BF16, kind="ExternalOutput").ap()
    k = KB(nc)
    emit_A2(k, io)
    k.build()
    return nc


def prep_A2_inputs(inputs, x1f, x1b):
    oh = np.zeros((NE, NE * 128), dtype=BF)
    for e in range(NE):
        oh[e, e * 128:(e + 1) * 128] = 1.0
    return {
        "x1f": x1f, "x1b": x1b,
        "router_W": inputs["L0_ffn_router_W"],
        "router_b": inputs["L0_ffn_router_b"].reshape(1, NE),
        "sW1": inputs["L0_ffn_sW1"], "sW2": inputs["L0_ffn_sW2"],
        "eW1": inputs["L0_ffn_eW1"].reshape(-1),
        "eW2": inputs["L0_ffn_eW2"].reshape(-1),
        "sb1p": _plane(inputs["L0_ffn_sb1"]),
        "sb2p": _plane(inputs["L0_ffn_sb2"]),
        "eb1p": np.ascontiguousarray(
            inputs["L0_ffn_eb1"].reshape(NE, NC, 128).transpose(2, 0, 1)
            .reshape(128, NE * NC)),
        "eb2": inputs["L0_ffn_eb2"],
        "oh": oh,
    }



T2 = 256
NT2 = T // T2


def emit_A3(k, io):
    nc = k.nc

    x2bf = [k.sb(f"x2bf{c}", [128, T], BF16) for c in range(NC)]
    xtok = [k.sb(f"xtok{i}", [128, D], BF16) for i in range(2)]
    M_sb = [k.sb(f"Msb{i}", [128, 256], F32) for i in range(8)]
    M_bf = [k.sb(f"Mbf{i}", [128, 256], BF16) for i in range(8)]
    W2sb = [k.sb(f"W2sb{i}", [128, 256], BF16) for i in range(32)]
    qw_sb = [k.sb(f"qwsb{i}", [128, D], BF16) for i in range(8)]
    comb_t = [k.sb(f"combt{i}", [128, 256], BF16) for i in range(2)]
    Q_bf = [k.sb(f"Qbf{i}", [128, T2], BF16) for i in range(8)]
    zrow3 = [k.sb(f"zrow3_{d}", [128, TT], F32) for d in range(NC)]
    xres3 = [k.sb(f"xres3_{i}", [128, T2], F32) for i in range(2)]
    h_full = [k.sb(f"hfull{c}", [128, TT], F32) for c in range(NC)]
    x3tb = [k.sb(f"x3tb{i}", [128, TT], BF16) for i in range(2)]
    gplane = k.sb("gplane_t", [128, NC], F32)
    bplane = k.sb("bplane_t", [128, NC], F32)
    centT = k.sb("centT_t", [128, NC * 8], F32)
    sc_t = k.sb("sc_t", [128, 8], F32)

    pb = [k.ps(f"pb{i}", [128, TT], F32) for i in range(8)]
    ln = PostLN(k, "ln3", dict(sum=pb[4][0:1, :], sq=pb[5][0:1, :],
                               b1=pb[6][:], b2=pb[6][:]))
    scps = pb[7][:, 0:8]

    e_qw = []
    for a in range(4):
        for ec in range(2):
            e_qw.append(k.dma("gpsimd", qw_sb[a * 2 + ec][:],
                              io["queries"][(a * 256 + ec * 128) * D:
                                            (a * 256 + ec * 128 + 128) * D]
                              .rearrange("(p j) -> p j", p=128)))
    e_g = k.dma("sync", gplane[:], io["gplane"])
    e_b = k.dma("sync", bplane[:], io["bplane"])
    e_cent = k.dma("sync", centT[:], io["centT"])
    e_x2 = []
    for c in range(NC):
        e_x2.append(k.dma("sync", x2bf[c][:],
                          io["x2b"][c * 128:(c + 1) * 128, :]))

    e_mz = [k.op("vector", lambda e, i=i: e.memset(M_sb[i][:], 0.0))
            for i in range(8)]
    e_tp_use = [None, None]
    e_madd = {i: e_mz[i] for i in range(8)}
    e_bankM = [None, None]
    for u in range(T // 128):
        par = u % 2
        xt_ = xtok[par]
        e_tp = []
        for c in range(NC):
            e_tp.append(k.dma("scalar", xt_[:, c * 128:(c + 1) * 128],
                              x2bf[c][:, u * 128:(u + 1) * 128], transpose=True,
                              wait=[e_x2[c], e_tp_use[par]]))
        for q in range(4):
            for f0 in range(2):
                i = q * 2 + f0
                bank = pb[i % 2][:, 0:256]
                e_mm = k.op("tensor", lambda e, q=q, f0=f0, xt_=xt_, bank=bank:
                            e.matmul(bank, xt_[:, q * 256 + f0 * 128:
                                               q * 256 + f0 * 128 + 128],
                                     xt_[:, q * 256:q * 256 + 256],
                                     start=True, stop=True),
                            wait=[e_tp, e_bankM[i % 2]])
                e_ad = k.op("vector", lambda e, i=i, bank=bank: e.tensor_add(
                    M_sb[i][:], M_sb[i][:], bank),
                    wait=[e_mm, e_madd[i]])
                e_madd[i] = e_ad
                e_bankM[i % 2] = e_ad
        e_tp_use[par] = e_mm
    e_mcast = []
    for i in range(8):
        e_mcast.append(k.op("vector", lambda e, i=i: e.tensor_copy(
            M_bf[i][:], M_sb[i][:]), wait=[e_madd[i]]))
    e_bankQ_init = list(e_bankM)

    e_w2 = {}
    e_comb_use = [None, None]
    e_bankW = [None, None]
    for a in range(4):
        for q in range(4):
            for fc in range(2):
                idx = (a * 4 + q) * 2 + fc
                bank = pb[2 + idx % 2][:, 0:256]
                e_mm = None
                for g0 in range(2):
                    cpar = g0
                    src = io["combiners"][(a * 1024 + q * 256 + g0 * 128) * 256:
                                          (a * 1024 + q * 256 + g0 * 128 + 128) * 256]
                    src = src.rearrange("(p j) -> p j", p=128)
                    e_c = k.dma("gpsimd", comb_t[cpar][:], src,
                                wait=[e_comb_use[cpar]])
                    e_mm = k.op("tensor", lambda e, q=q, fc=fc, g0=g0, bank=bank,
                                ct=comb_t[cpar]: e.matmul(
                                    bank, M_bf[q * 2 + g0][:, fc * 128:fc * 128 + 128],
                                    ct[:], start=(g0 == 0), stop=(g0 == 1)),
                                wait=[e_c, e_mcast[q * 2 + g0]] +
                                     ([e_bankW[idx % 2]] if g0 == 0 else []))
                    e_comb_use[cpar] = e_mm
                widx = (a * 4 + q) * 2 + fc
                e_ev = k.op("vector", lambda e, widx=widx, bank=bank: e.tensor_copy(
                    W2sb[widx][:], bank), wait=[e_mm])
                e_w2[(a, q, fc)] = e_ev
                e_bankW[idx % 2] = e_ev

    e_zadds = {}
    e_qbf_use = {}
    e_bankQ = e_bankQ_init
    e_bankO = [e_w2[(3, 3, 0)], e_w2[(3, 3, 1)]]
    e_x3_dma = {}
    e_h_read = None
    e_sc_prev = None
    e_o3_prev = None
    e_xres_prev = [None, None]
    e_stats_prev = None

    def prev2(d_, t_):
        return (t_, d_ - 2) if d_ >= 2 else (t_ - 1, d_ + NC - 2)

    for t2 in range(NT2):
        tt = t2 // 2
        sub = t2 % 2
        for a in range(4):
            for q in range(4):
                for fc in range(2):
                    bank = pb[q % 2][:, 0:T2]
                    e_mm = None
                    for ec in range(2):
                        e_mm = k.op("tensor", lambda e, a=a, q=q, fc=fc, ec=ec,
                                    bank=bank, t2=t2: e.matmul(
                                        bank,
                                        qw_sb[a * 2 + ec][:, q * 256 + fc * 128:
                                                          q * 256 + fc * 128 + 128],
                                        x2bf[2 * a + ec][:, t2 * T2:(t2 + 1) * T2],
                                        start=(ec == 0), stop=(ec == 1)),
                                    wait=[e_qw[a * 2 + ec], e_x2[2 * a + ec]] +
                                         ([e_bankQ[q % 2]] if ec == 0 else []))
                    e_ev = k.op("vector", lambda e, q=q, fc=fc, bank=bank:
                                e.tensor_copy(Q_bf[q * 2 + fc][:], bank),
                                wait=[e_mm, e_qbf_use.get(q * 2 + fc)])
                    e_bankQ[q % 2] = e_ev
                    e_qbf_use[q * 2 + fc] = e_ev
            for g1 in range(2):
                bank = pb[2 + g1][:, 0:T2]
                e_mm = None
                for q in range(4):
                    for fc in range(2):
                        e_mm = k.op("tensor", lambda e, a=a, q=q, fc=fc, g1=g1,
                                    bank=bank: e.matmul(
                                        bank,
                                        W2sb[(a * 4 + q) * 2 + fc][:,
                                                                   g1 * 128:
                                                                   g1 * 128 + 128],
                                        Q_bf[q * 2 + fc][:],
                                        start=(q == 0 and fc == 0),
                                        stop=(q == 3 and fc == 1)),
                                    wait=[e_w2[(a, q, fc)],
                                          e_qbf_use[q * 2 + fc]] +
                                         ([e_bankO[g1]] if q == 0 and fc == 0
                                          else []))
                dch = 2 * a + g1
                xr = xres3[dch % 2]
                e_x = k.dma("scalar", xr[:],
                            io["x2f"][dch * 128:(dch + 1) * 128,
                                      t2 * T2:(t2 + 1) * T2],
                            wait=[e_xres_prev[dch % 2]])
                e_z = k.op("vector", lambda e, dch=dch, sub=sub, bank=bank, xr=xr:
                           e.tensor_add(zrow3[dch][:, sub * T2:(sub + 1) * T2],
                                        bank, xr[:]),
                           wait=[e_mm, e_x, e_stats_prev])
                e_xres_prev[dch % 2] = e_z
                e_bankO[g1] = e_z
                e_zadds[(t2, dch)] = e_z
        if sub == 0:
            continue

        h_evs = []

        def sink(dout, xt, ev, tt=tt, h_evs=h_evs):
            o1 = k.dma("sync", io["x3f"][dout * 128:(dout + 1) * 128,
                                         tt * TT:(tt + 1) * TT],
                       xt[:], wait=[ev])
            e_h = k.op("vector", lambda e, dout=dout, xt=xt: e.tensor_scalar(
                h_full[dout][:], xt[:], gplane[:, dout:dout + 1],
                bplane[:, dout:dout + 1], ALU.mult, ALU.add),
                wait=[ev, e_g, e_b, e_h_read])
            h_evs.append(e_h)
            xtb = x3tb[dout % 2]
            e_hb = k.op("vector", lambda e, dout=dout, xtb=xtb: e.tensor_copy(
                xtb[:], h_full[dout][:]),
                wait=[e_h, e_x3_dma.get(prev2(dout, tt) + (1,))])
            o2 = k.dma("sync", io["hb"][dout * 128:(dout + 1) * 128,
                                        tt * TT:(tt + 1) * TT],
                       xtb[:], wait=[e_hb])
            e_x3_dma[(tt, dout, 1)] = o2
            return [o1, e_h]

        e_sr, norm_evs = ln.tile(
            [zrow3[d_][:, 0:TT] for d_ in range(NC)],
            [[e_zadds[(tt * 2, d_)], e_zadds[(tt * 2 + 1, d_)]]
             for d_ in range(NC)], sink)
        e_stats_prev = e_sr

        for ts in range(4):
            e_mm = None
            for c in range(NC):
                e_mm = k.op("tensor", lambda e, c=c, ts=ts: e.matmul(
                    scps, h_full[c][:, ts * 128:(ts + 1) * 128],
                    centT[:, c * 8:(c + 1) * 8],
                    start=(c == 0), stop=(c == NC - 1)),
                    wait=[e_cent, h_evs] +
                         ([e_sc_prev] if c == 0 else []))
            e_sc = k.op("vector", lambda e: e.tensor_copy(sc_t[:], scps),
                        wait=[e_mm, e_o3_prev], strict=True, fence=True)
            e_sc_prev = e_sc
            o3 = k.dma("sync",
                       io["scores"][(tt * TT + ts * 128) * 8:
                                    (tt * TT + (ts + 1) * 128) * 8]
                       .rearrange("(p j) -> p j", p=128),
                       sc_t[:], wait=[e_sc])
            e_o3_prev = o3
            e_mm_last = e_mm
        e_h_read = e_mm_last

    k.op("vector", lambda e: e.engine_nop(),
         wait=[k.queue_done("sync"), k.queue_done("gpsimd"),
               k.queue_done("scalar")])


def build_A3(dbg=False):
    nc = bass.Bass("TRN2", target_bir_lowering=False)
    io = {}
    io["x2f"] = nc.dram_tensor("x2f", [D, T], F32, kind="ExternalInput").ap()
    io["x2b"] = nc.dram_tensor("x2b", [D, T], BF16, kind="ExternalInput").ap()
    io["queries"] = nc.dram_tensor("queries", [4 * 256 * D], F32,
                                   kind="ExternalInput").ap()
    io["combiners"] = nc.dram_tensor("combiners", [4 * D * 256], F32,
                                     kind="ExternalInput").ap()
    io["gplane"] = nc.dram_tensor("gplane", [128, NC], F32, kind="ExternalInput").ap()
    io["bplane"] = nc.dram_tensor("bplane", [128, NC], F32, kind="ExternalInput").ap()
    io["centT"] = nc.dram_tensor("centT", [128, NC * 8], F32,
                                 kind="ExternalInput").ap()
    io["x3f"] = nc.dram_tensor("x3f", [D, T], F32, kind="ExternalOutput").ap()
    io["hb"] = nc.dram_tensor("hb", [D, T], BF16, kind="ExternalOutput").ap()
    io["scores"] = nc.dram_tensor("scores", [T * 8], F32, kind="ExternalOutput").ap()
    k = KB(nc)
    emit_A3(k, io)
    k.build()
    return nc


def prep_A3_inputs(inputs, x2f, x2b):
    centT = np.ascontiguousarray(inputs["L1_ffn_centroids"].T)
    centp = np.ascontiguousarray(
        centT.reshape(NC, 128, 8).transpose(1, 0, 2).reshape(128, NC * 8))
    return {
        "x2f": x2f, "x2b": x2b,
        "queries": inputs["L1_attn_queries"].reshape(-1),
        "combiners": inputs["L1_attn_combiners"].reshape(-1),
        "gplane": _plane(inputs["L1_ffn_ln_g"]),
        "bplane": _plane(inputs["L1_ffn_ln_b"]),
        "centT": centp,
    }



CAP = 768
NSL = CAP // 128
CGLOB = 4096.0
BIGF = 8388608.0
NEG = -1.0e38
TAU_IT = 32
TH_IT = 17


def emit_B(k, io):
    nc = k.nc
    _regs = {}

    def breg(e, val):
        if val not in _regs:
            _regs[val] = e.to_reg(val)
        return _regs[val]

    scw = [k.sb(f"scw{e}", [128, 256], F32) for e in range(8)]
    taken = k.sb("taken", [128, 256], F32)
    iotag = k.sb("iotag_t", [128, 256], F32)
    s_av = k.sb("s_av", [128, 256], F32)
    s_mn = k.sb("s_mn", [128, 256], F32)
    selg = k.sb("selg", [128, 256], F32)
    eqm = k.sb("eqm", [128, 256], F32)
    tjunk = k.sb("tjunk", [128, 256], F32)
    ones256 = k.sb("ones256", [128, 256], F32)
    cntc = k.sb("cntc", [128, 1], F32)
    red1 = k.sb("red1", [128, 1], F32)
    rrow_d = k.sb("rrow_d", [1, 128], F32)
    sc_lo = [k.sb(f"sc_lo{i}", [1, 1], F32) for i in range(2)]
    sc_hi = [k.sb(f"sc_hi{i}", [1, 1], F32) for i in range(2)]
    sc_mid = k.sb("sc_mid", [1, 1], F32)
    sc_t1 = k.sb("sc_t1", [1, 1], F32)
    sc_t2 = k.sb("sc_t2", [1, 1], F32)
    sc_cmp = k.sb("sc_cmp", [1, 1], F32)
    sc_m = k.sb("sc_m", [1, 1], F32)
    ones_row_b = k.sb("ones_row_b", [1, 128], F32)
    ones_col_b = k.sb("ones_col_b", [128, 1], F32)
    ltri = k.sb("ltri_t", [128, 128], F32)
    eyeb = k.sb("eyeb_t", [128, 128], BF16)
    tlid_f = k.sb("tlidf_t", [128, 32], F32)
    tidf = [k.sb(f"tidf{i}", [128, 128], F32) for i in range(2)]
    fidxf = [k.sb(f"fidxf{i}", [128, 1], F32) for i in range(2)]
    incl = k.sb("incl", [128, 32], F32)
    excl = k.sb("excl", [128, 32], F32)
    dbgrow = k.sb("dbgrow", [1, 32], F32) if io.get("dbg") is not None else None
    posr = k.sb("posr", [128, 32], F32)
    posall = k.sb("posall", [128, 32], F32)
    pos_i = k.sb("pos_i", [128, 32], I32)
    basec = k.sb("basec", [128, 1], F32)

    w1sb = [k.sb(f"w1sb{c}", [128, 4096], BF16) for c in range(NC)]
    w2t = [k.sb(f"w2t{i}", [128, D], BF16) for i in range(4)]
    relu1 = [k.sb(f"rl{i}", [128, 256], BF16) for i in range(32)]
    h_rows = [k.sb(f"hrow{i}", [128, D], BF16) for i in range(2)]
    x3_rows = [k.sb(f"x3row{i}", [128, D], F32) for i in range(2)]
    h_fm = [k.sb(f"hfm{c}", [128, 256], BF16) for c in range(NC)]
    idxc = [k.sb(f"idxc{i}", [128, 1], I32) for i in range(2)]
    b1p = k.sb("b1p_t", [128, 256], F32)
    b2sb = k.sb("b2sb", [128, D], F32)
    b2row = k.sb("b2row_t", [1, D], F32)
    z_sb = [k.sb(f"zsb{i}", [128, D], F32) for i in range(2)]
    zt = [k.sb(f"zt{i}", [128, D], F32) for i in range(2)]
    ot = [k.sb(f"ot{i}", [128, D], F32) for i in range(2)]
    fsc = k.sb("fsc", [128, D], F32)
    fsum = k.sb("fsum", [128, 1], F32)
    fsq = k.sb("fsq", [128, 1], F32)
    fm = k.sb("fm", [128, 1], F32)
    fv = k.sb("fv", [128, 1], F32)
    fr = k.sb("fr", [128, 1], F32)
    eps_colB = k.sb("eps_colB", [128, 1], F32)

    py1 = [k.ps(f"py1_{i}", [128, TT], F32) for i in range(2)]
    py2 = [k.ps(f"py2_{i}", [128, TT], F32) for i in range(4)]
    pbis = k.ps("pbis", [128, TT], F32)
    ptp = k.ps("ptp", [128, 128], BF16)
    midb = pbis[:, 0:1]
    cnt1 = pbis[0:1, 4:5]
    baseps = pbis[:, 8:9]

    e_sc = [k.dma("sync", scw[e][:], io["scoresR"][e * 128 * 256:(e + 1) * 128 * 256]
                  .rearrange("(p f) -> p f", p=128)) for e in range(8)]
    e_io = k.dma("sync", iotag[:], io["iotag"])
    e_lt = k.dma("sync", ltri[:], io["ltri"])
    e_ey = k.dma("sync", eyeb[:], io["eyeb"])
    e_tlf = k.dma("sync", tlid_f[:], io["tlid_f"])
    e_b1 = k.dma("sync", b1p[:], io["b1p"])
    e_orow = k.op("vector", lambda e: e.memset(ones_row_b[:], 1.0))
    half_row = k.sb("half_row", [1, 128], F32)
    e_hrow = k.op("vector", lambda e: e.memset(half_row[:], 0.5))
    e_epsB = k.op("vector", lambda e: e.memset(eps_colB[:], LN_EPS))
    e_o256 = k.op("vector", lambda e: e.memset(ones256[:], 1.0))
    e_ocol = k.op("vector", lambda e: e.memset(ones_col_b[:], 1.0))
    e_tk = k.op("vector", lambda e: e.memset(taken[:], 0.0))
    e_pa = k.op("vector", lambda e: e.memset(posall[:], BIGF))



    def vop(fn, wait=()):
        return k.op("vector", fn, wait=wait, strict=True, fence=True)

    def colred(src_ap, op, wait):
        e1 = vop(lambda e: e.tensor_reduce(red1[:], src_ap, AX.X, op), wait)
        e2 = k.dma("gpsimd", io["bounce"].rearrange("(p a) -> p a", a=1),
                   red1[:], wait=[e1])
        e3 = k.dma("gpsimd", rrow_d[:], io["bounce"].rearrange("(a p) -> a p", a=1),
                   wait=[e2])
        e4 = vop(lambda e: e.tensor_reduce(sc_t1[0:1, 0:1], rrow_d[:], AX.X, op),
                 [e3])
        return e4

    e_selprev = None
    e_posall = e_pa
    e_scatter_done = []
    e_sgd = None
    for rnd in range(8):
        if rnd < 7:
            ea = vop(lambda e, rnd=rnd: e.tensor_mul(tjunk[:], scw[rnd][:], taken[:]),
                     [e_sc[rnd], e_tk, e_selprev])
            eb = vop(lambda e, rnd=rnd: e.tensor_sub(s_av[:], scw[rnd][:], tjunk[:]))
            ec = vop(lambda e: e.scalar_tensor_tensor(
                s_av[:], taken[:], NEG, s_av[:], ALU.mult, ALU.add))
            ed = vop(lambda e, rnd=rnd: e.scalar_tensor_tensor(
                s_mn[:], taken[:], 1.0e38, scw[rnd][:], ALU.mult, ALU.add))
            e_hi0 = colred(s_av[:], ALU.max, [ec])
            eh = vop(lambda e: e.tensor_copy(sc_hi[0][:], sc_t1[0:1, 0:1]), [e_hi0])
            e_lo0 = colred(s_mn[:], ALU.min, [ed, eh])
            el = vop(lambda e: e.tensor_scalar_add(sc_lo[0][:], sc_t1[0:1, 0:1],
                                                   -1.0), [e_lo0])
            e_it = [el, eh]
            e_cntev = None
            for it in range(TAU_IT):
                pl, ph = sc_lo[it % 2], sc_hi[it % 2]
                nl, nh = sc_lo[1 - it % 2], sc_hi[1 - it % 2]
                em1 = vop(lambda e, pl=pl, ph=ph: e.tensor_add(sc_mid[:], pl[:],
                                                               ph[:]), e_it)
                eb1 = k.op("tensor", lambda e: e.matmul(
                    midb, half_row[:], sc_mid[:], start=True, stop=True),
                    wait=[em1, e_hrow, e_cntev])
                ed1 = vop(lambda e, pl=pl: e.tensor_sub(
                    sc_t1[0:1, 0:1], midb[0:1, 0:1], pl[:]), [eb1])
                ed2 = vop(lambda e, ph=ph: e.tensor_sub(
                    sc_t2[:], ph[:], midb[0:1, 0:1]))
                ecn = vop(lambda e: e.scalar_tensor_tensor(
                    tjunk[:], s_av[:], midb, ones256[:], ALU.is_gt, ALU.mult,
                    accum_out=cntc[:]), [eb1, ec, e_o256])
                ecm = k.op("tensor", lambda e: e.matmul(
                    cnt1, cntc[:], ones_col_b[:], start=True, stop=True),
                    wait=[ecn, e_ocol])
                e_cntev = ecm
                ecp = vop(lambda e: e.tensor_single_scalar(
                    sc_cmp[:], cnt1, CGLOB, ALU.is_ge), [ecm])
                eu3 = vop(lambda e, pl=pl, nl=nl: e.scalar_tensor_tensor(
                    nl[:], sc_t1[0:1, 0:1], sc_cmp[0:1, 0:1], pl[:],
                    ALU.mult, ALU.add), [ecp, ed1])
                eu6 = vop(lambda e, nh=nh: e.scalar_tensor_tensor(
                    nh[:], sc_t2[0:1, 0:1], sc_cmp[0:1, 0:1], midb[0:1, 0:1],
                    ALU.mult, ALU.add), [ed2])
                e_it = [eu3, eu6]
            tau = sc_hi[TAU_IT % 2]
            eb2 = k.op("tensor", lambda e, tau=tau: e.matmul(
                midb, ones_row_b[:], tau[:], start=True, stop=True),
                wait=e_it + [e_cntev])
            eg1 = vop(lambda e: e.scalar_tensor_tensor(
                selg[:], s_av[:], midb, ones256[:], ALU.is_gt, ALU.mult,
                accum_out=cntc[:]), [eb2, e_o256, e_sgd])
            eg2 = k.op("tensor", lambda e: e.matmul(
                cnt1, cntc[:], ones_col_b[:], start=True, stop=True), wait=[eg1])
            egm = vop(lambda e: e.tensor_scalar(
                sc_m[:], cnt1, -1.0, CGLOB, ALU.mult, ALU.add), [eg2])
            ee1 = vop(lambda e: e.tensor_scalar(
                eqm[:], s_av[:], midb, None, ALU.is_equal), [eb2, eg1])
            if io.get("dbg") is not None:
                vop(lambda e, rnd=rnd, tau=tau: e.tensor_copy(
                    dbgrow[0:1, rnd * 4:rnd * 4 + 1], tau[:]), [eb2])
                vop(lambda e, rnd=rnd: e.tensor_copy(
                    dbgrow[0:1, rnd * 4 + 1:rnd * 4 + 2], sc_m[:]), [egm])
            ei1 = vop(lambda e: e.memset(sc_lo[0][:], -1.0), [eb2, eg2])
            ei2 = vop(lambda e: e.memset(sc_hi[0][:], 40000.0), [eb2, eg2])
            e_it = [ei1, ei2, ee1, egm, e_io]
            e_cntev = eg2
            for it in range(TH_IT):
                pl, ph = sc_lo[it % 2], sc_hi[it % 2]
                nl, nh = sc_lo[1 - it % 2], sc_hi[1 - it % 2]
                em1 = vop(lambda e, pl=pl, ph=ph: e.tensor_add(sc_mid[:], pl[:],
                                                               ph[:]), e_it)
                eb1 = k.op("tensor", lambda e: e.matmul(
                    midb, half_row[:], sc_mid[:], start=True, stop=True),
                    wait=[em1, e_hrow, e_cntev])
                ed1 = vop(lambda e, pl=pl: e.tensor_sub(
                    sc_t1[0:1, 0:1], midb[0:1, 0:1], pl[:]), [eb1])
                ed2 = vop(lambda e, ph=ph: e.tensor_sub(
                    sc_t2[:], ph[:], midb[0:1, 0:1]))
                ecn = vop(lambda e: e.scalar_tensor_tensor(
                    tjunk[:], iotag[:], midb, eqm[:], ALU.is_le, ALU.mult,
                    accum_out=cntc[:]), [eb1])
                ecm = k.op("tensor", lambda e: e.matmul(
                    cnt1, cntc[:], ones_col_b[:], start=True, stop=True),
                    wait=[ecn])
                e_cntev = ecm
                ecp = vop(lambda e: e.tensor_single_scalar(
                    sc_cmp[:], cnt1, sc_m[0:1, 0:1], ALU.is_lt), [ecm])
                eu3 = vop(lambda e, pl=pl, nl=nl: e.scalar_tensor_tensor(
                    nl[:], sc_t1[0:1, 0:1], sc_cmp[0:1, 0:1], pl[:],
                    ALU.mult, ALU.add), [ecp, ed1])
                eu6 = vop(lambda e, nh=nh: e.scalar_tensor_tensor(
                    nh[:], sc_t2[0:1, 0:1], sc_cmp[0:1, 0:1], midb[0:1, 0:1],
                    ALU.mult, ALU.add), [ed2])
                e_it = [eu3, eu6]
            th = sc_hi[TH_IT % 2]
            eb3 = k.op("tensor", lambda e, th=th: e.matmul(
                midb, ones_row_b[:], th[:], start=True, stop=True),
                wait=e_it + [e_cntev])
            es1 = vop(lambda e: e.scalar_tensor_tensor(
                tjunk[:], iotag[:], midb, eqm[:], ALU.is_le, ALU.mult), [eb3])
            es2 = vop(lambda e: e.tensor_add(selg[:], selg[:], tjunk[:]))
            e_sel = es2
            if io.get("dbg") is not None:
                vop(lambda e, rnd=rnd, th=th: e.tensor_copy(
                    dbgrow[0:1, rnd * 4 + 2:rnd * 4 + 3], th[:]), [eb3])
        else:
            e_sel = vop(lambda e: e.tensor_scalar(
                selg[:], taken[:], -1.0, 1.0, ALU.mult, ALU.add),
                [e_selprev, e_sgd])
        if io.get("dbg") is not None:
            e_sgd = k.dma("gpsimd", io["dbg"]["selg"]
                          [rnd * 32768:(rnd + 1) * 32768]
                          .rearrange("(p f) -> p f", p=128), selg[:],
                          wait=[e_sel])
        e_tk = vop(lambda e: e.tensor_add(taken[:], taken[:], selg[:]), [e_sel])
        e_selprev = e_tk
        sel_loc = selg[:, 0:32]
        ec1 = vop(lambda e: e.tensor_tensor_scan(
            incl[:], ones256[:, 0:32], sel_loc, 0.0,
            ALU.mult, ALU.add), [e_sel, e_o256])
        ec2 = vop(lambda e: e.tensor_sub(excl[:], incl[:], sel_loc))
        ecb = k.op("tensor", lambda e: e.matmul(
            baseps, ltri[:], incl[:, 31:32], start=True, stop=True),
            wait=[ec1, e_lt])
        e_basec = vop(lambda e: e.tensor_copy(basec[:], baseps), [ecb])
        ec3 = vop(lambda e, rnd=rnd: e.tensor_scalar(
            posr[:], excl[:], basec[:, 0:1], float(rnd * CAP), ALU.add, ALU.add),
            [e_basec, ec2])
        ec4 = vop(lambda e: e.scalar_tensor_tensor(
            posr[:], sel_loc, -BIGF, posr[:], ALU.mult, ALU.add), [ec3])
        ec5 = vop(lambda e: e.tensor_scalar_add(posr[:], posr[:], BIGF))
        ec6 = vop(lambda e: e.tensor_tensor(posall[:], posall[:], posr[:], ALU.min),
                  [ec5, e_posall])
        e_posall = ec6
    dbg = io.get("dbg")
    if dbg is not None:
        k.dma("gpsimd", dbg["posall"], posall[:], wait=[e_posall])
        k.dma("gpsimd", dbg["incl"], incl[:], wait=[e_posall])
        k.dma("gpsimd", dbg["basec"], basec[:], wait=[e_posall])
        k.dma("gpsimd", dbg["tauscal"], dbgrow[:],
              wait=k.queue_done("vector"))
    ep1 = vop(lambda e: e.tensor_copy(pos_i[:], posall[:]), [e_posall])
    e_bigz = vop(lambda e: e.memset(fsc[:], BIGF), [])
    e_z0 = vop(lambda e: e.memset(z_sb[0][:], 0.0), [])
    e_z0b = vop(lambda e: e.memset(h_rows[0][:], 0.0), [])
    e_init = []
    for i in range(6):
        e_init.append(k.dma("sync", io["bucket_tid"][i * 131072:(i + 1) * 131072]
                            .rearrange("(p j) -> p j", p=128), fsc[:],
                            wait=[e_bigz]))
    for i in range(48):
        e_init.append(k.dma("sync", io["bucket_x3"][i * 131072:(i + 1) * 131072]
                            .rearrange("(p j) -> p j", p=128), z_sb[0][:],
                            wait=[e_z0]))
    for i in range(48):
        e_init.append(k.dma("sync", io["bucket_h"][i * 131072:(i + 1) * 131072]
                            .rearrange("(p j) -> p j", p=128), h_rows[0][:],
                            wait=[e_z0b]))
    e_scat_prev = {}
    for f in range(32):
        par = f % 2
        e_lh = k.dma("sync", h_rows[par][:], io["htm"][f * 128:(f + 1) * 128, :],
                     wait=[e_scat_prev.get(("h", par)), e_init])
        e_lx = k.dma("sync", x3_rows[par][:], io["x3tm"][f * 128:(f + 1) * 128, :],
                     wait=[e_scat_prev.get(("x", par)), e_init])
        e_tf = vop(lambda e, par=par, f=f: e.tensor_scalar_mul(
            tidf[par][:], ones256[:, 0:128], tlid_f[:, f:f + 1]),
            [e_tlf, e_o256, e_scat_prev.get(("t", par))])
        e_s1 = k.op("gpsimd", lambda e, par=par, f=f: e.indirect_dma_start(
            io["bucket_h"].rearrange("(r j) -> r j", j=D),
            bass.IndirectOffsetOnAxis(pos_i[:, f:f + 1], 0),
            h_rows[par][:], None,
            bounds_check=breg(e, 8 * CAP - 1), oob_is_err=False),
            wait=[ep1, e_lh, e_init], amt=16)
        e_s2 = k.op("gpsimd", lambda e, par=par, f=f: e.indirect_dma_start(
            io["bucket_x3"].rearrange("(r j) -> r j", j=D),
            bass.IndirectOffsetOnAxis(pos_i[:, f:f + 1], 0),
            x3_rows[par][:], None,
            bounds_check=breg(e, 8 * CAP - 1), oob_is_err=False),
            wait=[ep1, e_lx, e_init], amt=16)
        e_s3 = k.op("gpsimd", lambda e, par=par, f=f: e.indirect_dma_start(
            io["bucket_tid"].rearrange("(r j) -> r j", j=128),
            bass.IndirectOffsetOnAxis(pos_i[:, f:f + 1], 0),
            tidf[par][:], None,
            bounds_check=breg(e, 8 * CAP - 1), oob_is_err=False),
            wait=[ep1, e_tf, e_init], amt=16)
        e_scat_prev[("h", par)] = e_s1
        e_scat_prev[("x", par)] = e_s2
        e_scat_prev[("t", par)] = e_s3
        e_scatter_done = [e_s1, e_s2, e_s3]

    e_w1use = None
    e_w2use = [None] * 4
    e_gat_use = {}
    e_idx_use = {}
    e_hfm_use = None
    e_b2b = None
    e_zdma = {0: e_init, 1: e_init}
    e_y1bank = [None, None]
    e_y2bank = [None] * 4
    e_tp_done = None
    for ei in range(8):
        e_w1 = []
        for c in range(NC):
            e_w1.append(k.dma("gpsimd", w1sb[c][:],
                              io["w1"][(ei * D + c * 128) * 4096:
                                       (ei * D + c * 128 + 128) * 4096]
                              .rearrange("(p j) -> p j", p=128),
                              wait=[e_w1use]))
        e_b2r = k.dma("sync", b2row[:], io["b2"][ei * D:(ei + 1) * D]
                      .rearrange("(a j) -> a j", a=1), wait=[e_b2b])
        e_bb = []
        for hh in range(2):
            e_bb.append(k.op("tensor", lambda e, hh=hh: e.matmul(
                py1[hh][:], ones_row_b[:], b2row[:, hh * TT:(hh + 1) * TT],
                start=True, stop=True),
                wait=[e_b2r, e_orow, e_y1bank[hh]]))
        e_b2c = []
        for hh in range(2):
            e_b2c.append(k.op("vector", lambda e, hh=hh: e.tensor_copy(
                b2sb[:, hh * TT:(hh + 1) * TT], py1[hh][:]), wait=[e_bb[hh]]))
        e_y1bank = list(e_b2c)
        e_b2b = e_b2c

        e_y2_last = None
        e_gx_by = {}
        for pair in range(3):
            sts = [pair * 2, pair * 2 + 1]
            e_relu = {}
            e_tpc = {c: [] for c in range(NC)}
            for j, st in enumerate(sts):
                par = st % 2
                e_gh = k.dma("sync", h_rows[par][:],
                             io["bucket_h"][(ei * CAP + st * 128) * D:
                                            (ei * CAP + (st + 1) * 128) * D]
                             .rearrange("(p j) -> p j", p=128),
                             wait=[e_scatter_done, e_gat_use.get(("h", par))])
                e_gx = k.dma("sync", x3_rows[par][:],
                             io["bucket_x3"][(ei * CAP + st * 128) * D:
                                             (ei * CAP + (st + 1) * 128) * D]
                             .rearrange("(p j) -> p j", p=128),
                             wait=[e_scatter_done, e_gat_use.get(("x", par))])
                e_gx_by[par] = e_gx
                for c in range(NC):
                    e_t1 = k.op("tensor", lambda e, c=c, par=par: e.transpose(
                        ptp[:], h_rows[par][:, c * 128:(c + 1) * 128], eyeb[:]),
                        wait=[e_gh, e_ey, e_tp_done])
                    e_t2 = k.op("vector", lambda e, c=c, j=j: e.tensor_copy(
                        h_fm[c][:, j * 128:(j + 1) * 128], ptp[:]),
                        wait=[e_t1, e_hfm_use])
                    e_tp_done = e_t2
                    e_tpc[c].append(e_t2)
                e_gat_use[("h", par)] = e_tpc[NC - 1][-1]
            for hd in range(32):
                bank = py1[hd % 2]
                e_mm = None
                for c in range(NC):
                    e_mm = k.op("tensor", lambda e, c=c, hd=hd, bank=bank:
                                e.matmul(bank[:, 0:256],
                                         w1sb[c][:, hd * 128:(hd + 1) * 128],
                                         h_fm[c][:],
                                         start=(c == 0), stop=(c == NC - 1)),
                                wait=[e_w1[c], e_tpc[c]] +
                                     ([e_y1bank[hd % 2]] if c == 0 else []))
                e_ev = k.op("scalar", lambda e, hd=hd, ei=ei, bank=bank:
                            e.activation(relu1[hd][:], bank[:, 0:256],
                                         AF.Gelu_apprx_tanh,
                                         bias=b1p[:, ei * 32 + hd:
                                                  ei * 32 + hd + 1]),
                            wait=[e_mm, e_b1, e_y2_last])
                e_relu[hd] = e_ev
                e_y1bank[hd % 2] = e_ev
            e_hfm_use = e_mm
            e_mm4 = [None] * 4
            for hd in range(32):
                wt = w2t[hd % 4]
                e_w2 = k.dma("gpsimd", wt[:],
                             io["w2"][(ei * 4096 + hd * 128) * D:
                                      (ei * 4096 + hd * 128 + 128) * D]
                             .rearrange("(p j) -> p j", p=128),
                             wait=[e_w2use[hd % 4]])
                for j in range(2):
                    for hh in range(2):
                        gi = j * 2 + hh
                        e_mm4[gi] = k.op("tensor", lambda e, j=j, hh=hh, hd=hd,
                                         wt=wt, gi=gi: e.matmul(
                                             py2[gi][:],
                                             relu1[hd][:, j * 128:(j + 1) * 128],
                                             wt[:, hh * TT:(hh + 1) * TT],
                                             start=(hd == 0), stop=(hd == 31)),
                                         wait=[e_w2, e_relu[hd]] +
                                              ([e_y2bank[gi]] if hd == 0 else []))
                e_w2use[hd % 4] = e_mm4[3]
            e_y2_last = e_mm4[3]
            for j, st in enumerate(sts):
                par = st % 2
                zb = z_sb[par]
                ezz = []
                for hh in range(2):
                    gi = j * 2 + hh
                    sl = slice(hh * TT, (hh + 1) * TT)
                    e_z1 = k.op("vector", lambda e, par=par, gi=gi, sl=sl, zb=zb:
                                e.scalar_tensor_tensor(
                                    zb[:, sl], x3_rows[par][:, sl], 2.0,
                                    py2[gi][:], ALU.mult, ALU.add),
                                wait=[e_mm4[gi], e_gx_by[par],
                                      e_zdma.get(par)])
                    e_z2 = k.op("vector", lambda e, sl=sl, zb=zb: e.tensor_add(
                        zb[:, sl], zb[:, sl], b2sb[:, sl]),
                        wait=[e_z1, e_b2c])
                    ezz.append(e_z2)
                    e_y2bank[gi] = e_z1
                e_zd = k.dma("sync", io["z_d"][(ei * CAP + st * 128) * D:
                                               (ei * CAP + (st + 1) * 128) * D]
                             .rearrange("(p j) -> p j", p=128), zb[:],
                             wait=ezz)
                e_zdma[par] = e_zd
                e_gat_use[("x", par)] = ezz[-1]
        e_w1use = e_mm
    e_f_prev = [None, None]
    e_oprev = [None, None]
    for ei in range(8):
        for st in range(NSL):
            par = st % 2
            e_zl = k.dma("scalar", zt[par][:],
                         io["z_d"][(ei * CAP + st * 128) * D:
                                   (ei * CAP + (st + 1) * 128) * D]
                         .rearrange("(p j) -> p j", p=128),
                         wait=[k.queue_done("sync"), e_f_prev[par]])
            e_ilf = k.dma("scalar", fidxf[par][:],
                          io["bucket_tid"].rearrange("(r j) -> r j", j=128)
                          [ei * CAP + st * 128: ei * CAP + (st + 1) * 128, 0:1],
                          wait=[e_idx_use.get(par), e_oprev[par],
                                e_scatter_done],
                          allow_slow_non_contiguous=True)
            e_il = vop(lambda e, par=par: e.tensor_copy(idxc[par][:],
                                                        fidxf[par][:]), [e_ilf])
            e_idx_use[par] = e_il
            e_s1 = vop(lambda e, par=par: e.reduce_sum(fsum[:], zt[par][:],
                                                       axis=AX.X), [e_zl])
            e_s2 = vop(lambda e, par=par: e.scalar_tensor_tensor(
                fsc[:], zt[par][:], 1.0, zt[par][:], ALU.mult, ALU.mult,
                accum_out=fsq[:]), [e_zl])
            e_m1 = vop(lambda e: e.tensor_scalar_mul(fm[:], fsum[:], 1.0 / D),
                       [e_s1])
            e_v1 = vop(lambda e: e.tensor_scalar_mul(fv[:], fsq[:], 1.0 / D),
                       [e_s2])
            e_v2 = vop(lambda e: e.tensor_mul(fr[:], fm[:], fm[:]), [e_m1])
            e_v3 = vop(lambda e: e.tensor_sub(fv[:], fv[:], fr[:]), [e_v2, e_v1])
            e_l1 = k.op("scalar", lambda e: e.activation(
                fr[:], fv[:], AF.Ln, bias=eps_colB[:, 0:1]),
                wait=[e_v3, e_epsB])
            e_l2 = k.op("scalar", lambda e: e.activation(
                fr[:], fr[:], AF.Exp, scale=-0.5))
            e_n1 = vop(lambda e, par=par: e.tensor_scalar(
                ot[par][:], zt[par][:], fm[:, 0:1], fr[:, 0:1],
                ALU.subtract, ALU.mult), [e_l2, e_oprev[par]])
            e_sc2 = k.op("gpsimd", lambda e, par=par: e.indirect_dma_start(
                io["out_tm"],
                bass.IndirectOffsetOnAxis(idxc[par][:, 0:1], 0),
                ot[par][:], None,
                bounds_check=breg(e, T - 1), oob_is_err=False),
                wait=[e_n1, e_il], amt=16)
            e_oprev[par] = e_sc2
            e_f_prev[par] = e_n1
    k.op("vector", lambda e: e.engine_nop(),
         wait=[k.queue_done("sync"), k.queue_done("gpsimd"),
               k.queue_done("scalar")])


def build_B(dbg=False):
    nc = bass.Bass("TRN2", target_bir_lowering=False)
    io = {}
    if dbg:
        io["dbg"] = {
            "posall": nc.dram_tensor("dbg_posall", [128, 32], F32,
                                     kind="ExternalOutput").ap(),
            "incl": nc.dram_tensor("dbg_incl", [128, 32], F32,
                                   kind="ExternalOutput").ap(),
            "basec": nc.dram_tensor("dbg_basec", [128, 1], F32,
                                    kind="ExternalOutput").ap(),
            "tauscal": nc.dram_tensor("dbg_tauscal", [1, 32], F32,
                                      kind="ExternalOutput").ap(),
            "selg": nc.dram_tensor("dbg_selg", [8 * 128 * 256], F32,
                                   kind="ExternalOutput").ap(),
        }
    io["scoresR"] = nc.dram_tensor("scoresR", [8 * 128 * 256], F32,
                                   kind="ExternalInput").ap()
    io["iotag"] = nc.dram_tensor("iotag", [128, 256], F32, kind="ExternalInput").ap()
    io["ltri"] = nc.dram_tensor("ltri", [128, 128], F32, kind="ExternalInput").ap()
    io["eyeb"] = nc.dram_tensor("eyeb", [128, 128], BF16, kind="ExternalInput").ap()
    io["tlid_f"] = nc.dram_tensor("tlid_f", [128, 32], F32, kind="ExternalInput").ap()
    io["b1p"] = nc.dram_tensor("b1p", [128, 256], F32, kind="ExternalInput").ap()
    io["b2"] = nc.dram_tensor("b2", [8 * D], F32, kind="ExternalInput").ap()
    io["w1"] = nc.dram_tensor("w1", [8 * D * 4096], F32, kind="ExternalInput").ap()
    io["w2"] = nc.dram_tensor("w2", [8 * 4096 * D], F32, kind="ExternalInput").ap()
    io["htm"] = nc.dram_tensor("htm", [T, D], BF16, kind="ExternalInput").ap()
    io["x3tm"] = nc.dram_tensor("x3tm", [T, D], F32, kind="ExternalInput").ap()
    io["out_tm"] = nc.dram_tensor("out_tm", [T, D], F32, kind="ExternalOutput").ap()
    io["bucket_h"] = nc.dram_tensor("bucket_h", [8 * CAP * D], BF16,
                                    kind="Internal").ap()
    io["bucket_x3"] = nc.dram_tensor("bucket_x3", [8 * CAP * D], F32,
                                     kind="Internal").ap()
    io["bucket_tid"] = nc.dram_tensor("bucket_tid", [8 * CAP * 128], F32,
                                      kind="ExternalOutput" if dbg else "Internal").ap()
    io["z_d"] = nc.dram_tensor("z_d", [8 * CAP * D], F32, kind="Internal").ap()
    io["bounce"] = nc.dram_tensor("bounce", [128], F32, kind="Internal").ap()
    k = KB(nc)
    emit_B(k, io)
    k.build()
    return nc


def prep_B_inputs(inputs, core, scores_all, x3tm, htm):
    own = np.arange(core * T, (core + 1) * T)
    rest = np.concatenate([np.arange(0, core * T),
                           np.arange((core + 1) * T, NCORES * T)])
    ordc = np.concatenate([own, rest])
    S = scores_all[ordc].reshape(256, 128, 8).transpose(2, 1, 0)
    iotag = np.ascontiguousarray(
        ordc.reshape(256, 128).T).astype(np.float32)
    ltri = np.tril(np.ones((128, 128), dtype=np.float32), -1).T
    ltri = np.ascontiguousarray(ltri)
    tlid = np.ascontiguousarray(
        (np.arange(32)[None, :] * 128 + np.arange(128)[:, None])).astype(np.int32)
    b1 = inputs["L1_ffn_b1"]
    b1p = np.ascontiguousarray(
        b1.reshape(8, 32, 128).transpose(2, 0, 1).reshape(128, 256)).astype(np.float32)
    return {
        "scoresR": np.ascontiguousarray(S.reshape(-1)).astype(np.float32),
        "iotag": iotag,
        "ltri": ltri,
        "eyeb": np.eye(128, dtype=np.float32).astype(BF),
        "tlid_f": tlid.astype(np.float32),
        "b1p": b1p,
        "b2": inputs["L1_ffn_b2"].reshape(-1),
        "w1": inputs["L1_ffn_w1"].reshape(-1),
        "w2": inputs["L1_ffn_w2"].reshape(-1),
        "htm": htm,
        "x3tm": x3tm,
    }



def _plane(v):
    return np.ascontiguousarray(v.reshape(NC, 128).T).astype(np.float32)


def prep_A1_inputs(inputs, core):
    x = inputs["L0_attn_x"][core]
    return {
        "x_cm": np.ascontiguousarray(x.T),
        "wkT": np.ascontiguousarray(inputs["L0_attn_Wk"].T),
        "wvT": np.ascontiguousarray(inputs["L0_attn_Wv"].T),
        "wrT": np.ascontiguousarray(inputs["L0_attn_Wr"].T),
        "woT": np.ascontiguousarray(inputs["L0_attn_Wo"].T),
        "tmk": _plane(inputs["L0_attn_time_mix_k"].reshape(-1)),
        "tmv": _plane(inputs["L0_attn_time_mix_v"].reshape(-1)),
        "tmr": _plane(inputs["L0_attn_time_mix_r"].reshape(-1)),
        "tdec": _plane(inputs["L0_attn_time_decay"]),
        "tfir": _plane(inputs["L0_attn_time_first"]),
    }


def kernel(**inputs):
    cores = list(range(NCORES))

    nc1 = build_A1()
    res1 = run_bass_kernel_spmd(nc1, [prep_A1_inputs(inputs, c) for c in cores],
                                core_ids=cores)
    nc2 = build_A2()
    maps2 = [prep_A2_inputs(inputs, np.asarray(res1.results[c]["x1f"]),
                            np.asarray(res1.results[c]["x1b"]))
             for c in cores]
    res2 = run_bass_kernel_spmd(nc2, maps2, core_ids=cores)
    nc3 = build_A3()
    maps3 = [prep_A3_inputs(inputs, np.asarray(res2.results[c]["x2f"]),
                            np.asarray(res2.results[c]["x2b"]))
             for c in cores]
    res3 = run_bass_kernel_spmd(nc3, maps3, core_ids=cores)
    scores_all = np.concatenate(
        [np.asarray(res3.results[c]["scores"]).reshape(T, 8) for c in cores])
    x3tm = [np.ascontiguousarray(np.asarray(res3.results[c]["x3f"]).T)
            for c in cores]
    htm = [np.ascontiguousarray(np.asarray(res3.results[c]["hb"]).T)
           for c in cores]
    ncb = build_B()
    mapsb = [prep_B_inputs(inputs, c, scores_all, x3tm[c], htm[c])
             for c in cores]
    resb = run_bass_kernel_spmd(ncb, mapsb, core_ids=cores)
    return np.stack([np.asarray(resb.results[c]["out_tm"]) for c in cores])
